# version 06
# baseline (speedup 1.0000x reference)
import numpy as np
from contextlib import ExitStack
import concourse.bass as bass
import concourse.mybir as mybir

F32 = mybir.dt.float32
F16 = mybir.dt.float16
F8 = mybir.dt.float8e4
AF = mybir.ActivationFunctionType
OP = mybir.AluOpType

D = 1024
Hh = 4096
DS = 64
SPAN = 4096
TT = 512
NTILE = SPAN // TT
CH = 128
NCH = TT // CH
KD = D // 128
KH = Hh // 128
LR = float(np.float32(0.01))
INV_D = float(np.float32(1.0) / np.float32(1024.0))
EPS = float(np.float32(1e-5))
W = 8
HTT = TT // 2


def build_program(stage="full", sim1=False):
    nc = bass.Bass("TRN2", target_bir_lowering=False)
    x_d = nc.dram_tensor("x", [KD, 128, SPAN], F32, kind="ExternalInput")
    mk_d = nc.dram_tensor("mask", [KD, 128, SPAN], F16, kind="ExternalInput")
    w1_d = nc.dram_tensor("w1t", [KD, 128, Hh], F32, kind="ExternalInput")
    w2_d = nc.dram_tensor("w2t", [KH, 128, D], F32, kind="ExternalInput")
    wv_d = nc.dram_tensor("wvt", [KD, 128, D], F32, kind="ExternalInput")
    wk_d = nc.dram_tensor("wkt", [KD, 128, DS], F32, kind="ExternalInput")
    wq_d = nc.dram_tensor("wqt", [KD, 128, DS], F32, kind="ExternalInput")
    bh_d = nc.dram_tensor("bias_h", [128, KH], F32, kind="ExternalInput")
    vd_d = nc.dram_tensor("vecd", [128, 24], F32, kind="ExternalInput")
    be_d = nc.dram_tensor("bias_e", [DS, 2], F32, kind="ExternalInput")
    bv_d = nc.dram_tensor("bvrow", [1, D], F32, kind="ExternalInput")
    bk_d = nc.dram_tensor("bkrow", [1, DS], F32, kind="ExternalInput")
    cst_d = nc.dram_tensor("consts", [128, 576], F32, kind="ExternalInput")
    i64_d = nc.dram_tensor("i64h", [DS, DS], F16, kind="ExternalInput")
    lam_d = nc.dram_tensor("lam", [DS, W], F32, kind="ExternalInput")
    lamI_d = nc.dram_tensor("lamI", [DS, W * DS], F16, kind="ExternalInput")
    out_d = nc.dram_tensor("out", [KD, 128, SPAN], F32, kind="ExternalOutput")

    sprp_d = nc.dram_tensor("sp_rp", [KD, 128, SPAN], F16)
    spqh_d = nc.dram_tensor("sp_qh", [NTILE * NCH, DS, CH], F16)
    spct_d = nc.dram_tensor("sp_ct", [NTILE * NCH, DS, D], F16)
    ccin_d = nc.dram_tensor("cc_in", [DS, DS + D], F16)
    ccout_d = nc.dram_tensor("cc_out", [W * DS, DS + D], F16)

    es = ExitStack()
    def sb(name, shape, dt):
        return es.enter_context(nc.sbuf_tensor(name, shape, dt))
    def psum(name, shape, dt=F32):
        return es.enter_context(nc.psum_tensor(name, shape, dt))
    def sem(name):
        return es.enter_context(nc.semaphore(name))

    with es:
        W1Q = sb("W1Q", [128, KD * Hh], F8)
        W2Q = sb("W2Q", [128, KH * D], F8)
        WVT = sb("WVT", [128, KD * D], F16)
        WKT = sb("WKT", [128, KD * DS], F16)
        WQT = sb("WQT", [128, KD * DS], F16)
        B1Q = sb("B1Q", [128, KH], F32)
        B8 = sb("B8", [128, KH], F8)
        B2Q = sb("B2Q", [128, KD], F32)
        B28 = sb("B28", [128, KD], F8)
        GAM = sb("GAM", [128, KD], F32)
        BET = sb("BET", [128, KD], F32)
        BKE = sb("BKE", [DS, 2], F32)
        BVR = sb("BVR", [1, D], F16)
        BKR = sb("BKR", [1, DS], F16)
        ML = sb("ML", [128, 128], F16)
        MU = sb("MU", [128, 128], F16)
        TWOI = sb("TWOI", [128, 128], F16)
        I128F = sb("I128F", [128, 128], F16)
        I64F = sb("I64F", [DS, DS], F32)
        I64H = sb("I64H", [DS, DS], F16)
        LAM = sb("LAM", [DS, W], F32)
        LAMI = sb("LAMI", [DS, W * DS], F16)
        ONE16 = sb("ONE16", [128, 1], F16)
        ONE1F = sb("ONE1F", [1, 128], F32)
        ONE1H = sb("ONE1H", [1, 128], F16)
        EPSA = sb("EPSA", [1, 1], F32)
        ROWS = sb("ROWS", [1, 4 * TT], F32)
        XF = sb("XF", [128, KD * TT], F32)
        XB = sb("XB", [128, KD * TT], F16)
        MKS = sb("MKS", [128, 2 * TT], F16)
        LQ = sb("LQ", [128, KD * TT], F8)
        H8 = sb("H8", [128, KH * TT], F8)
        H16T = sb("H16T", [128, 2 * TT], F16)
        Y16T = sb("Y16T", [128, 2 * TT], F16)
        F32T = sb("F32T", [128, 2 * TT], F32)
        TMPA = sb("TMPA", [128, 2 * CH], F16)
        TMPB = sb("TMPB", [128, 2 * CH], F16)
        TMPF = sb("TMPF", [128, 2 * HTT], F32)
        RBF = sb("RBF", [128, 2 * HTT], F32)
        RB16 = sb("RB16", [128, 2], F16)
        HA = sb("HA", [128, KD * TT], F16)
        RP = sb("RP", [128, KD * TT], F16)
        VSB = sb("VSB", [128, 2 * D], F16)
        KSB = sb("KSB", [DS, 2 * CH], F16)
        QSB = sb("QSB", [DS, 2 * CH], F16)
        KTM = sb("KTM", [128, 2 * DS], F16)
        TB = sb("TB", [128, 2 * 128], F16)
        MSB = sb("MSB", [128, 2 * 128], F16)
        MTSB = sb("MTSB", [128, 2 * 128], F16)
        HB = sb("HB", [128, 2 * 128], F16)
        WTT = sb("WTT", [128, 2 * 128], F16)
        UV = sb("UV", [128, 2 * D], F16)
        UK = sb("UK", [128, 2 * DS], F16)
        QH = sb("QH", [DS, CH], F16)
        PC = sb("PC", [DS, NTILE * NCH * DS], F16)
        CTC = sb("CTC", [DS, D], F16)
        SPB = sb("SPB", [DS, DS], F16)
        SPTB = sb("SPTB", [DS, DS], F16)
        SCTB = sb("SCTB", [DS, D], F16)
        ZSB = sb("ZSB", [DS, D], F16)
        ZC1 = sb("ZC1", [DS, D], F16)
        ZH = sb("ZH", [DS, (NCH - 1) * D], F16)
        QHL = sb("QHL", [DS, NCH * CH], F16)
        CTL = sb("CTL", [DS, 2 * D], F16)
        GB = CTL[0:DS, 0:DS + D]
        PBL = QH[0:DS, 0:DS]
        CBL = CTC

        pA = [psum("pA0", [128, TT]), psum("pA1", [128, TT])]
        pB = [psum("pB0", [128, TT]), psum("pB1", [128, TT])]
        pC = psum("pC", [128, D])
        pD = psum("pD", [128, 512])
        pE = psum("pE", [128, 512])
        pE2A = pD[:, 384:512]

        s_w = sem("s_w"); s_wg = sem("s_wg"); s_x = sem("s_x")
        s_sp = sem("s_sp"); s_r = sem("s_r")
        s_rc0 = sem("s_rc0"); s_rc1 = sem("s_rc1")
        s_mk0 = sem("s_mk0"); s_mk1 = sem("s_mk1")
        s_mk2 = sem("s_mk2"); s_mk3 = sem("s_mk3")
        s_o = sem("s_o"); s_cc = sem("s_cc")
        s_t = sem("s_t"); s_v = sem("s_v"); s_s = sem("s_s"); s_g = sem("s_g")

        cnt = {"t": 0, "v": 0, "s": 0, "g": 0, "w": 0, "wg": 0, "x": 0, "sp": 0,
               "r": 0, "o": 0, "rc0": 0, "rc1": 0, "mk0": 0, "mk1": 0,
               "mk2": 0, "mk3": 0}
        SEMS = {"t": s_t, "v": s_v, "s": s_s, "g": s_g, "w": s_w, "wg": s_wg,
                "x": s_x, "sp": s_sp, "r": s_r, "o": s_o, "rc0": s_rc0, "rc1": s_rc1,
                "mk0": s_mk0, "mk1": s_mk1, "mk2": s_mk2, "mk3": s_mk3}

        def tick(instr, key, n=1):
            cnt[key] += n
            instr.then_inc(SEMS[key], n)
            return cnt[key]

        blk = es.enter_context(nc.Block())

        @blk.gpsimd
        def _(g):
            for kd in range(KD):
                tick(g.dma_start(out=W1Q[:, kd * Hh:(kd + 1) * Hh], in_=w1_d[kd]), "wg", 16)
            for kh in range(KH):
                if kh == 16:
                    g.wait_ge(s_wg, cnt["wg"])
                tick(g.dma_start(out=W2Q[:, kh * D:(kh + 1) * D], in_=w2_d[kh]), "wg", 16)
            g.wait_ge(s_wg, cnt["wg"])
            for kd in range(KD):
                tick(g.dma_start(out=WVT[:, kd * D:(kd + 1) * D], in_=wv_d[kd]), "wg", 16)
            for kd in range(KD):
                tick(g.dma_start(out=WKT[:, kd * DS:(kd + 1) * DS], in_=wk_d[kd]), "wg", 16)
                tick(g.dma_start(out=WQT[:, kd * DS:(kd + 1) * DS], in_=wq_d[kd]), "wg", 16)
            tick(g.dma_start(out=B8[:], in_=bh_d[:]), "wg", 16)
            tick(g.dma_start(out=B28[:], in_=vd_d[:, 0:KD]), "wg", 16)
            tick(g.dma_start(out=BVR[:], in_=bv_d[:]), "wg", 16)
            tick(g.dma_start(out=BKR[:], in_=bk_d[:]), "wg", 16)
            tick(g.dma_start(out=LAMI[:], in_=lamI_d[:]), "wg", 16)
            tick(g.dma_start(out=ML[:], in_=cst_d[:, 0:128]), "wg", 16)
            tick(g.dma_start(out=MU[:], in_=cst_d[:, 128:256]), "wg", 16)
            tick(g.dma_start(out=TWOI[:], in_=cst_d[:, 256:384]), "wg", 16)
            tick(g.dma_start(out=I128F[:], in_=cst_d[:, 384:512]), "wg", 16)

        @blk.sync
        def _(sy):
            tick(sy.dma_start(out=GAM[:], in_=vd_d[:, 8:16]), "w", 16)
            tick(sy.dma_start(out=BET[:], in_=vd_d[:, 16:24]), "w", 16)
            tick(sy.dma_start(out=BKE[:], in_=be_d[:]), "w", 16)
            tick(sy.dma_start(out=I64F[:], in_=cst_d[0:DS, 512:576]), "w", 16)
            tick(sy.dma_start(out=I64H[:], in_=i64_d[:]), "w", 16)
            tick(sy.dma_start(out=LAM[:], in_=lam_d[:]), "w", 16)
        w_done = cnt["w"]
        wg_done = cnt["wg"]

        @blk.scalar
        def _(s):
            s.preload_act_table(AF.Sqrt)

        @blk.vector
        def _(v):
            v.wait_ge(s_w, w_done)
            v.wait_ge(s_wg, wg_done)
            v.memset(ONE16[:], 1.0)
            v.memset(RB16[:], 0.0)
            v.memset(ONE1F[:], 1.0)
            v.memset(ONE1H[:], 1.0)
            v.memset(EPSA[:], EPS)
            v.tensor_copy(B1Q[:], B8[:])
            v.tensor_copy(B2Q[:], B28[:])
            v.tensor_copy(SPB[:], I64H[:])
            v.tensor_copy(SPTB[:], I64H[:])
            tick(v.memset(SCTB[:], 0.0), "v")
        init_v = cnt["v"]

        rel = {"LQ0": ("t", 0), "LQ1": ("t", 0), "XB": ("t", 0),
               "H80": ("t", 0), "H81": ("t", 0)}

        def emit_ln_stats(waits, src, pump=None):
            xb_rel = rel["XB"]
            @blk.vector
            def _(v):
                for k, val in waits:
                    v.wait_ge(SEMS[k], val)
                v.wait_ge(SEMS[xb_rel[0]], xb_rel[1])
                for kd in range(KD):
                    i = v.tensor_scalar(XB[:, kd * TT:(kd + 1) * TT],
                                        src[:, kd * TT:(kd + 1) * TT],
                                        float(1.0 / 64.0), None, OP.mult)
                tick(i, "v")
            v_copy = cnt["v"]
            if pump: pump()

            s_prior = cnt["s"]
            @blk.tensor
            def _(t):
                t.wait_ge(s_v, v_copy)
                t.wait_ge(s_s, s_prior)
                for kd in range(KD):
                    i = t.matmul(pE[0:1, 0:TT], ONE16[:], XB[:, kd * TT:(kd + 1) * TT],
                                 start=(kd == 0), stop=(kd == KD - 1))
                tick(i, "t")
            rel["XB"] = ("t", cnt["t"])
            mmtick = [0]
            for p in range(KD // 2):
                @blk.vector
                def _(v):
                    if p == 0:
                        v.wait_ge(s_v, v_copy)
                    else:
                        v.wait_ge(s_t, mmtick[0])
                    v.tensor_tensor(H16T[:, 0:TT], XB[:, (2 * p) * TT:(2 * p + 1) * TT],
                                    XB[:, (2 * p) * TT:(2 * p + 1) * TT], OP.mult)
                    tick(v.tensor_tensor(H16T[:, TT:2 * TT],
                                         XB[:, (2 * p + 1) * TT:(2 * p + 2) * TT],
                                         XB[:, (2 * p + 1) * TT:(2 * p + 2) * TT],
                                         OP.mult), "v")
                v_sqp = cnt["v"]
                @blk.tensor
                def _(t):
                    t.wait_ge(s_v, v_sqp)
                    t.matmul(pE[32:33, 0:TT], ONE16[:], H16T[:, 0:TT],
                             start=(p == 0), stop=False)
                    tick(t.matmul(pE[32:33, 0:TT], ONE16[:], H16T[:, TT:2 * TT],
                                  start=False, stop=(p == KD // 2 - 1)), "t")
                mmtick[0] = cnt["t"]
            t_sq = cnt["t"]
            if pump: pump()

            @blk.vector
            def _(v):
                v.wait_ge(s_t, t_sq)
                tick(v.tensor_scalar(ROWS[0:1, 0:TT], pE[0:1, 0:TT],
                                     float(64.0 / 1024.0), None, OP.mult), "v")
                v.wait_ge(s_v, cnt["v"])
                tick(v.tensor_tensor(ROWS[0:1, 2 * TT:3 * TT], ROWS[0:1, 0:TT],
                                     ROWS[0:1, 0:TT], OP.mult), "v")
                v.wait_ge(s_v, cnt["v"])
                tick(v.scalar_tensor_tensor(ROWS[0:1, TT:2 * TT], pE[32:33, 0:TT], 4.0,
                                            ROWS[0:1, 2 * TT:3 * TT],
                                            OP.mult, OP.subtract), "v")
            v_var = cnt["v"]

            @blk.scalar
            def _(s):
                s.wait_ge(s_v, v_var)
                tick(s.activation(ROWS[0:1, TT:2 * TT], ROWS[0:1, TT:2 * TT],
                                  AF.Sqrt, bias=EPSA[:, 0:1]), "s")
            s_sd = cnt["s"]

            @blk.vector
            def _(v):
                v.wait_ge(s_s, s_sd)
                tick(v.reciprocal(ROWS[0:1, 2 * TT:3 * TT], ROWS[0:1, TT:2 * TT]), "v")
                v.wait_ge(s_v, cnt["v"])
                tick(v.tensor_tensor(ROWS[0:1, 3 * TT:4 * TT], ROWS[0:1, 0:TT],
                                     ROWS[0:1, 2 * TT:3 * TT], OP.mult), "v")
            v_rows = cnt["v"]
            if pump: pump()
            return {"v_rows": v_rows, "src": src, "s_bc0": None, "v_ap0": None}

        def emit_ln_tail(out_kind, hh, ST, apply_waits=()):
            src = ST["src"]
            lq_rel = rel["LQ%d" % hh]
            @blk.tensor
            def _(t):
                if hh == 0:
                    t.wait_ge(s_v, ST["v_rows"])
                    t.wait_ge(s_s, cnt["s"])
                else:
                    t.wait_ge(s_s, ST["s_bc0"])
                t.matmul(pE[:, 0:HTT], ONE1F[:],
                         ROWS[0:1, 2 * TT + hh * HTT: 2 * TT + (hh + 1) * HTT],
                         start=True, stop=True)
                tick(t.matmul(pE[:, HTT:2 * HTT], ONE1F[:],
                              ROWS[0:1, 3 * TT + hh * HTT: 3 * TT + (hh + 1) * HTT],
                              start=True, stop=True), "t")
            t_bc = cnt["t"]

            @blk.scalar
            def _(s):
                s.wait_ge(s_t, t_bc)
                if hh == 1:
                    s.wait_ge(s_v, ST["v_ap0"])
                s.activation(RBF[:, 0:HTT], pE[:, 0:HTT], AF.Copy,
                             bias=0.0, scale=1.0)
                tick(s.activation(RBF[:, HTT:2 * HTT], pE[:, HTT:2 * HTT],
                                  AF.Copy, bias=0.0, scale=1.0), "s")
            s_bc = cnt["s"]
            if hh == 0:
                ST["s_bc0"] = s_bc

            @blk.vector
            def _(v):
                v.wait_ge(s_s, s_bc)
                if out_kind == "q":
                    v.wait_ge(SEMS[lq_rel[0]], lq_rel[1])
                for k, val in apply_waits:
                    v.wait_ge(SEMS[k], val)
                rstd = RBF[:, 0:HTT]
                mr = RBF[:, HTT:2 * HTT]
                tf = [TMPF[:, 0:HTT], TMPF[:, HTT:2 * HTT]]
                for kp in range(KD // 2):
                    kk = [2 * kp, 2 * kp + 1]
                    sls = [slice(kk[i] * TT + hh * HTT,
                                 kk[i] * TT + (hh + 1) * HTT) for i in (0, 1)]
                    if kp > 0 or hh == 1:
                        v.wait_ge(s_v, cnt["v"])
                    v.tensor_tensor(tf[0], src[:, sls[0]], rstd, OP.mult)
                    tick(v.tensor_tensor(tf[1], src[:, sls[1]], rstd, OP.mult), "v")
                    v.wait_ge(s_v, cnt["v"])
                    if out_kind == "q":
                        v.tensor_tensor(LQ[:, sls[0]], tf[0], mr, OP.subtract)
                        tick(v.tensor_tensor(LQ[:, sls[1]], tf[1], mr,
                                             OP.subtract), "v")
                    else:
                        v.tensor_tensor(tf[0], tf[0], mr, OP.subtract)
                        tick(v.tensor_tensor(tf[1], tf[1], mr, OP.subtract), "v")
                        v.wait_ge(s_v, cnt["v"])
                        for i in (0, 1):
                            v.tensor_tensor(RP[:, sls[i]], src[:, sls[i]], tf[i],
                                            OP.add)
                        v.tensor_scalar(HA[:, sls[0]], tf[0],
                                        GAM[:, kk[0]:kk[0] + 1],
                                        BET[:, kk[0]:kk[0] + 1], OP.mult, OP.add)
                        tick(v.tensor_scalar(HA[:, sls[1]], tf[1],
                                             GAM[:, kk[1]:kk[1] + 1],
                                             BET[:, kk[1]:kk[1] + 1],
                                             OP.mult, OP.add), "v")
            v_apply = cnt["v"]
            if hh == 0:
                ST["v_ap0"] = v_apply
            return v_apply

        mk_guard = {(p, h): ("v", 0) for p in (0, 1) for h in (0, 1)}
        def emit_ffn_half(tt, h, ln_v, res_src, res_inplace, evac_waits=(),
                          pump=None):
            co = h * HTT
            s_preh = cnt["s"]
            h8_rel = rel["H8%d" % h]
            tm = {}; sm = {}; vm = {}
            for kh in range(KH):
                @blk.tensor
                def _(t):
                    if kh == 0:
                        t.wait_ge(s_v, ln_v)
                    if kh < 2:
                        t.wait_ge(s_s, s_preh)
                    else:
                        t.wait_ge(s_s, sm[kh - 2])
                    bank = pA[kh % 2]
                    for kd in range(KD):
                        i = t.matmul(bank[:, co:co + HTT],
                                     W1Q[:, kd * Hh + kh * 128: kd * Hh + (kh + 1) * 128],
                                     LQ[:, kd * TT + co: kd * TT + co + HTT],
                                     start=(kd == 0), stop=(kd == KD - 1))
                    tick(i, "t")
                tm[kh] = cnt["t"]
                @blk.scalar
                def _(s):
                    s.wait_ge(s_t, tm[kh])
                    if kh >= 2:
                        s.wait_ge(s_v, vm[kh - 2])
                    tick(s.activation(H16T[:, (kh % 2) * TT + co:
                                           (kh % 2) * TT + co + HTT],
                                      pA[kh % 2][:, co:co + HTT], AF.Relu,
                                      bias=B1Q[:, kh:kh + 1], scale=1.0), "s")
                sm[kh] = cnt["s"]
                @blk.vector
                def _(v):
                    if kh == 0:
                        v.wait_ge(SEMS[h8_rel[0]], h8_rel[1])
                    v.wait_ge(s_s, sm[kh])
                    tick(v.tensor_copy(H8[:, kh * TT + co: kh * TT + co + HTT],
                                       H16T[:, (kh % 2) * TT + co:
                                            (kh % 2) * TT + co + HTT]), "v")
                vm[kh] = cnt["v"]
                if pump:
                    pump()
            rel["LQ%d" % h] = ("t", tm[KH - 1])

            tm2 = {}; sm2 = {}; vm2 = {}
            pend = [None]
            def emit_add(v, pd):
                sl = slice(pd * TT + co, pd * TT + co + HTT)
                pft = F32T[:, (pd % 2) * TT + co: (pd % 2) * TT + co + HTT]
                if res_inplace:
                    return v.tensor_tensor(XF[:, sl], XF[:, sl], pft, OP.add)
                return v.tensor_tensor(XF[:, sl], res_src[:, sl], pft, OP.add)
            for dv in range(KD):
                p2 = dv % 2
                mkkey = "mk%d" % (2 * p2 + h)
                g0 = mk_guard[(p2, h)]
                @blk.sync
                def _(sy):
                    sy.wait_ge(SEMS[g0[0]], g0[1])
                    tick(sy.dma_start(out=MKS[:, p2 * TT + co: p2 * TT + co + HTT],
                                      in_=mk_d[dv, :, tt * TT + co:
                                               tt * TT + co + HTT]), mkkey, 16)
                mk_ld = cnt[mkkey]
                @blk.tensor
                def _(t):
                    if dv == 0:
                        t.wait_ge(s_v, vm[KH - 1])
                    if dv < 2:
                        t.wait_ge(s_s, s_preh)
                    else:
                        t.wait_ge(s_s, sm2[dv - 2])
                    bank = pB[p2]
                    for kh in range(KH):
                        i = t.matmul(bank[:, co:co + HTT],
                                     W2Q[:, kh * D + dv * 128: kh * D + (dv + 1) * 128],
                                     H8[:, kh * TT + co: kh * TT + co + HTT],
                                     start=(kh == 0), stop=(kh == KH - 1))
                    tick(i, "t")
                tm2[dv] = cnt["t"]
                @blk.scalar
                def _(s):
                    s.wait_ge(s_t, tm2[dv])
                    if dv >= 2:
                        s.wait_ge(s_v, vm2[dv - 2])
                    tick(s.activation(Y16T[:, p2 * TT + co: p2 * TT + co + HTT],
                                      pB[p2][:, co:co + HTT], AF.Identity,
                                      bias=B2Q[:, dv:dv + 1], scale=1.0), "s")
                sm2[dv] = cnt["s"]
                @blk.vector
                def _(v):
                    if dv == 0:
                        for k, val in evac_waits:
                            v.wait_ge(SEMS[k], val)
                    v.wait_ge(s_s, sm2[dv])
                    v.wait_ge(SEMS[mkkey], mk_ld)
                    if dv > 0:
                        v.wait_ge(s_v, vm2[dv - 1])
                        emit_add(v, dv - 1)
                    tick(v.tensor_tensor(F32T[:, p2 * TT + co: p2 * TT + co + HTT],
                                         Y16T[:, p2 * TT + co: p2 * TT + co + HTT],
                                         MKS[:, p2 * TT + co: p2 * TT + co + HTT],
                                         OP.mult), "v")
                vm2[dv] = cnt["v"]
                mk_guard[(p2, h)] = ("v", cnt["v"])
                if pump:
                    pump()
            @blk.vector
            def _(v):
                v.wait_ge(s_v, vm2[KD - 1])
                tick(emit_add(v, KD - 1), "v")
            rel["H8%d" % h] = ("t", tm2[KD - 1])
            return cnt["v"]

        qhct_prev = [0]
        rp_prev = [0]
        span_s = [0]
        rp_tile_spill = [0] * NTILE
        vproj_last = [0]

        def pair_stages_p1(tt, pidx, ha_ready_v):
            PX = [0, 1]
            cidxs = [tt * NCH + 2 * pidx, tt * NCH + 2 * pidx + 1]
            chs = [2 * pidx, 2 * pidx + 1]
            csls = [[slice(kd * TT + ch * CH, kd * TT + (ch + 1) * CH)
                     for kd in range(KD)] for ch in chs]
            st = {}
            for j in (0, 1):
                par = PX[j]
                @blk.tensor
                def _(t):
                    if j == 0:
                        t.wait_ge(s_v, ha_ready_v)
                    t.wait_ge(s_v, vproj_last[0])
                    if span_s[0]:
                        t.wait_ge(s_s, span_s[0])
                    for half in range(2):
                        for kd in range(KD):
                            t.matmul(pC[0:CH, half * 512:(half + 1) * 512],
                                     HA[:, csls[j][kd]],
                                     WVT[:, kd * D + half * 512: kd * D + (half + 1) * 512],
                                     start=(kd == 0), stop=False)
                        i = t.matmul(pC[0:CH, half * 512:(half + 1) * 512], ONE1H[:],
                                     BVR[0:1, half * 512:(half + 1) * 512],
                                     start=False, stop=True)
                    for kd in range(KD):
                        t.matmul(pD[0:DS, 0:CH], WKT[:, kd * DS:(kd + 1) * DS],
                                 HA[:, csls[j][kd]], start=(kd == 0), stop=(kd == KD - 1))
                    for kd in range(KD):
                        t.matmul(pD[0:DS, CH:2 * CH], WQT[:, kd * DS:(kd + 1) * DS],
                                 HA[:, csls[j][kd]], start=(kd == 0), stop=(kd == KD - 1))
                    for kd in range(KD):
                        t.matmul(pD[0:CH, 320:320 + DS], HA[:, csls[j][kd]],
                                 WKT[:, kd * DS:(kd + 1) * DS], start=(kd == 0), stop=False)
                    tick(t.matmul(pD[0:CH, 320:320 + DS], ONE1H[:], BKR[:],
                                  start=False, stop=True), "t")
                st[("t_proj", j)] = cnt["t"]
                yield
                @blk.vector
                def _(v):
                    v.wait_ge(s_t, st[("t_proj", j)])
                    v.tensor_copy(VSB[:, par * D:par * D + 512], pC[0:CH, 0:512])
                    v.tensor_copy(VSB[:, par * D + 512:par * D + 1024], pC[0:CH, 512:1024])
                    v.tensor_scalar(KSB[:, par * CH:(par + 1) * CH], pD[0:DS, 0:CH],
                                    BKE[:, 0:1], None, OP.add)
                    v.tensor_scalar(QSB[:, par * CH:(par + 1) * CH], pD[0:DS, CH:2 * CH],
                                    BKE[:, 1:2], None, OP.add)
                    tick(v.tensor_copy(KTM[:, par * DS:(par + 1) * DS],
                                       pD[0:CH, 320:320 + DS]), "v")
                st[("v_proj", j)] = cnt["v"]
                vproj_last[0] = cnt["v"]
                yield
            @blk.tensor
            def _(t):
                t.wait_ge(s_v, st[("v_proj", 1)])
                t.matmul(pD[:, 256:384], KSB[:, 0:CH], KSB[:, 0:CH], start=True, stop=True)
                tick(t.matmul(pD[:, 384:512], KSB[:, CH:2 * CH], KSB[:, CH:2 * CH],
                              start=True, stop=True), "t")
            st["t_A"] = cnt["t"]
            yield
            @blk.vector
            def _(v):
                v.wait_ge(s_t, st["t_A"])
                for j in (0, 1):
                    g = pD[:, 256 + j * 128:384 + j * 128]
                    sl1 = slice(j * 128, (j + 1) * 128)
                    v.tensor_scalar(TMPA[:, sl1], g, LR, None, OP.mult)
                tick(v.tensor_copy(RB16[:, 0:1], RB16[:, 0:1]), "v")
                v.wait_ge(s_v, cnt["v"])
                for j in (0, 1):
                    sl1 = slice(j * 128, (j + 1) * 128)
                    sl2 = slice(j * 128, (j + 1) * 128)
                    v.tensor_tensor(TMPB[:, sl1], TMPA[:, sl1], ML[:], OP.mult)
                    v.tensor_tensor(TMPA[:, sl1], TMPA[:, sl1], MU[:], OP.mult)
                tick(v.tensor_copy(RB16[:, 0:1], RB16[:, 0:1]), "v")
                v.wait_ge(s_v, cnt["v"])
                for j in (0, 1):
                    sl1 = slice(j * 128, (j + 1) * 128)
                    v.tensor_tensor(TB[:, sl1], TMPB[:, sl1], I128F[:], OP.add)
                    v.tensor_tensor(MSB[:, sl1], I128F[:], TMPA[:, sl1], OP.subtract)
                    i = v.tensor_tensor(MTSB[:, sl1], I128F[:], TMPB[:, sl1], OP.subtract)
                tick(i, "v")
            prev = ("v", cnt["v"])
            yield
            for it in range(6):
                @blk.tensor
                def _(t):
                    t.wait_ge(SEMS[prev[0]], prev[1])
                    t.matmul(pD[:, 256:384], TB[:, 0:128], MSB[:, 0:128],
                             start=True, stop=True)
                    tick(t.matmul(pD[:, 384:512], TB[:, 128:256], MSB[:, 128:256],
                                  start=True, stop=True), "t")
                st["t_G"] = cnt["t"]
                yield
                @blk.vector
                def _(v):
                    v.wait_ge(s_t, st["t_G"])
                    v.scalar_tensor_tensor(HB[:, 0:128], pD[:, 256:384], -1.0, TWOI[:],
                                           OP.mult, OP.add)
                    tick(v.scalar_tensor_tensor(HB[:, 128:256], pD[:, 384:512], -1.0,
                                                TWOI[:], OP.mult, OP.add), "v")
                st["v_H"] = cnt["v"]
                @blk.tensor
                def _(t):
                    t.wait_ge(s_v, st["v_H"])
                    t.matmul(pD[:, 0:128], MTSB[:, 0:128], HB[:, 0:128],
                             start=True, stop=True)
                    t.matmul(pD[:, 128:256], HB[:, 0:128], MTSB[:, 0:128],
                             start=True, stop=True)
                    t.matmul(pD[:, 256:384], MTSB[:, 128:256], HB[:, 128:256],
                             start=True, stop=True)
                    tick(t.matmul(pD[:, 384:512], HB[:, 128:256], MTSB[:, 128:256],
                                  start=True, stop=True), "t")
                st["t_M"] = cnt["t"]
                yield
                @blk.scalar
                def _(s):
                    s.wait_ge(s_t, st["t_M"])
                    s.activation(MSB[:, 0:128], pD[:, 0:128], AF.Copy, bias=0.0, scale=1.0)
                    s.activation(MTSB[:, 0:128], pD[:, 128:256], AF.Copy, bias=0.0, scale=1.0)
                    s.activation(MSB[:, 128:256], pD[:, 256:384], AF.Copy, bias=0.0, scale=1.0)
                    tick(s.activation(MTSB[:, 128:256], pD[:, 384:512], AF.Copy,
                                      bias=0.0, scale=1.0), "s")
                prev = ("s", cnt["s"])
            yield
            for j in (0, 1):
                par = PX[j]
                cidx = cidxs[j]
                @blk.tensor
                def _(t):
                    t.wait_ge(SEMS[prev[0]], prev[1])
                    if j == 1:
                        t.wait_ge(s_s, span_s[0])
                        t.wait_ge(s_v, st[("v_y", 0)])
                    m = MSB[:, par * 128:(par + 1) * 128]
                    t.matmul(pC[0:CH, 0:512], m, VSB[:, par * D:par * D + 512],
                             start=True, stop=True)
                    t.matmul(pC[0:CH, 512:1024], m, VSB[:, par * D + 512:par * D + 1024],
                             start=True, stop=True)
                    t.matmul(pD[0:CH, 0:DS], m, KTM[:, par * DS:(par + 1) * DS],
                             start=True, stop=True)
                    tick(t.matmul(pD[:, 256 + par * 128:384 + par * 128],
                                  KSB[:, par * CH:(par + 1) * CH],
                                  QSB[:, par * CH:(par + 1) * CH],
                                  start=True, stop=True), "t")
                st[("t_U", j)] = cnt["t"]
                yield
                @blk.scalar
                def _(s):
                    s.wait_ge(s_t, st[("t_U", j)])
                    s.activation(UV[:, par * D:par * D + 512], pC[0:CH, 0:512],
                                 AF.Copy, bias=0.0, scale=LR)
                    s.activation(UV[:, par * D + 512:par * D + 1024], pC[0:CH, 512:1024],
                                 AF.Copy, bias=0.0, scale=LR)
                    tick(s.activation(UK[:, par * DS:(par + 1) * DS],
                                      pD[0:CH, 0:DS], AF.Copy, bias=0.0,
                                      scale=LR), "s")
                st[("s_U", j)] = cnt["s"]
                @blk.vector
                def _(v):
                    v.wait_ge(s_t, st[("t_U", j)])
                    tick(v.tensor_tensor(WTT[:, par * 128:(par + 1) * 128],
                                         pD[:, 256 + par * 128:384 + par * 128],
                                         MU[:], OP.mult), "v")
                st[("v_W", j)] = cnt["v"]
                yield
                for wave in range(2):
                    @blk.tensor
                    def _(t):
                        t.wait_ge(s_s, st[("s_U", j)])
                        t.wait_ge(s_v, st[("v_W", j)] if wave == 0 else st[("v_y", j)])
                        for q in range(4):
                            kd = wave * 4 + q
                            i = t.matmul(pC[:, q * 128:(q + 1) * 128],
                                         UV[:, par * D + kd * 128:par * D + (kd + 1) * 128],
                                         WTT[:, par * 128:(par + 1) * 128],
                                         start=True, stop=True)
                        tick(i, "t")
                    st[("t_y", j)] = cnt["t"]
                    yield
                    @blk.vector
                    def _(v):
                        v.wait_ge(s_t, st[("t_y", j)])
                        v.wait_ge(s_sp, rp_prev[0])
                        for q in range(4):
                            kd = wave * 4 + q
                            i = v.tensor_tensor(RP[:, csls[j][kd]], RP[:, csls[j][kd]],
                                                pC[:, q * 128:(q + 1) * 128], OP.add)
                        tick(i, "v")
                    st[("v_y", j)] = cnt["v"]
                yield
                @blk.tensor
                def _(t):
                    t.wait_ge(s_v, st[("v_y", j)])
                    if span_s[0]:
                        t.wait_ge(s_s, span_s[0])
                    t.matmul(pD[0:DS, 0:128], UK[:, par * DS:(par + 1) * DS],
                             WTT[:, par * 128:(par + 1) * 128], start=True, stop=True)
                    t.matmul(pD[0:DS, 128:128 + DS], UK[:, par * DS:(par + 1) * DS],
                             KTM[:, par * DS:(par + 1) * DS], start=True, stop=True)
                    t.matmul(pC[0:DS, 0:512], KTM[:, par * DS:(par + 1) * DS],
                             UV[:, par * D:par * D + 512], start=True, stop=True)
                    tick(t.matmul(pC[0:DS, 512:1024], KTM[:, par * DS:(par + 1) * DS],
                                  UV[:, par * D + 512:par * D + 1024],
                                  start=True, stop=True), "t")
                st[("t_QP", j)] = cnt["t"]
                yield
                @blk.vector
                def _(v):
                    v.wait_ge(s_t, st[("t_QP", j)])
                    v.wait_ge(s_sp, qhct_prev[0])
                    v.scalar_tensor_tensor(QH[:], pD[0:DS, 0:128], -1.0,
                                           QSB[:, par * CH:(par + 1) * CH],
                                           OP.mult, OP.add)
                    v.scalar_tensor_tensor(PC[:, cidx * DS:(cidx + 1) * DS],
                                           pD[0:DS, 128:128 + DS], -1.0, I64F[:],
                                           OP.mult, OP.add)
                    v.tensor_copy(CTC[:, 0:512], pC[0:DS, 0:512])
                    tick(v.tensor_copy(CTC[:, 512:1024], pC[0:DS, 512:1024]), "v")
                st[("v_QP", j)] = cnt["v"]
                yield
                @blk.gpsimd
                def _(g):
                    g.wait_ge(s_v, st[("v_QP", j)])
                    g.wait_ge(s_sp, cnt["sp"])
                    tick(g.dma_start(out=spqh_d[cidx], in_=QH[:]), "sp", 16)
                    tick(g.dma_start(out=spct_d[cidx], in_=CTC[:]), "sp", 16)
                qhct_prev[0] = cnt["sp"]
                @blk.tensor
                def _(t):
                    t.wait_ge(s_v, st[("v_QP", j)])
                    pcs = PC[:, cidx * DS:(cidx + 1) * DS]
                    t.matmul(pD[0:DS, 128:128 + DS], SPTB[:], pcs, start=True, stop=True)
                    t.matmul(pD[0:DS, 192:192 + DS], pcs, SPTB[:], start=True, stop=True)
                    t.matmul(pC[0:DS, 0:512], pcs, SCTB[:, 0:512], start=True, stop=False)
                    t.matmul(pC[0:DS, 0:512], I64H[:], CTC[:, 0:512], start=False, stop=True)
                    t.matmul(pC[0:DS, 512:1024], pcs, SCTB[:, 512:1024], start=True, stop=False)
                    tick(t.matmul(pC[0:DS, 512:1024], I64H[:], CTC[:, 512:1024],
                                  start=False, stop=True), "t")
                st[("t_span", j)] = cnt["t"]
                yield
                @blk.scalar
                def _(s):
                    s.wait_ge(s_t, st[("t_span", j)])
                    s.activation(SPB[:], pD[0:DS, 128:128 + DS], AF.Copy, bias=0.0, scale=1.0)
                    s.activation(SPTB[:], pD[0:DS, 192:192 + DS], AF.Copy, bias=0.0, scale=1.0)
                    s.activation(SCTB[:, 0:512], pC[0:DS, 0:512], AF.Copy, bias=0.0, scale=1.0)
                    tick(s.activation(SCTB[:, 512:1024], pC[0:DS, 512:1024], AF.Copy,
                                      bias=0.0, scale=1.0), "s")
                span_s[0] = cnt["s"]
                if j == 0:
                    yield
            if pidx == NCH // 2 - 1:
                yield
                @blk.gpsimd
                def _(g):
                    g.wait_ge(s_v, st[("v_y", 1)])
                    g.wait_ge(s_sp, cnt["sp"])
                    for kd in range(KD):
                        tick(g.dma_start(out=sprp_d[kd, :, tt * TT:(tt + 1) * TT],
                                         in_=RP[:, kd * TT:(kd + 1) * TT]), "sp", 16)
                rp_tile_spill[tt] = cnt["sp"]
                rp_prev[0] = cnt["sp"]

        pending = []
        def pump():
            if pending:
                try:
                    next(pending[0])
                except StopIteration:
                    pending.pop(0)
        def pump_all():
            while pending:
                try:
                    next(pending[0])
                except StopIteration:
                    pending.pop(0)

        ln2_prev = [init_v]
        for tt in range(NTILE):
            @blk.sync
            def _(sy):
                if tt == 0:
                    sy.wait_ge(s_w, w_done)
                    sy.wait_ge(s_wg, wg_done)
                    sy.wait_ge(s_v, init_v)
                else:
                    sy.wait_ge(s_v, ln2_prev[0])
                    sy.wait_ge(s_x, cnt["x"])
                for kd in range(KD):
                    tick(sy.dma_start(out=XF[:, kd * TT:(kd + 1) * TT],
                                      in_=x_d[kd, :, tt * TT:(tt + 1) * TT]), "x", 16)
            x_loaded = cnt["x"]

            st1 = emit_ln_stats([("x", x_loaded)], XF, pump=pump)
            a0 = emit_ln_tail("q", 0, st1)
            emit_ffn_half(tt, 0, a0, XF, True, pump=pump)
            a1 = emit_ln_tail("q", 1, st1)
            ffn1_v = emit_ffn_half(tt, 1, a1, XF, True, pump=pump)
            pump_all()
            st2 = emit_ln_stats([("v", ffn1_v)], XF)
            b0 = emit_ln_tail("ln2", 0, st2, apply_waits=[("sp", rp_prev[0])])
            pending.append(pair_stages_p1(tt, 0, b0))
            b1 = emit_ln_tail("ln2", 1, st2, apply_waits=[("sp", rp_prev[0])])
            pump(); pump()
            pending.append(pair_stages_p1(tt, 1, b1))
            ln2_prev[0] = b1
            if tt == NTILE - 1:
                pump_all()

        @blk.gpsimd
        def _(g):
            g.wait_ge(s_s, span_s[0])
            g.wait_ge(s_sp, cnt["sp"])
            tick(g.dma_start(out=ccin_d[:, 0:DS], in_=SPB[:]), "sp", 16)
            tick(g.dma_start(out=ccin_d[:, DS:DS + D], in_=SCTB[:]), "sp", 16)
        cc_staged = cnt["sp"]

        @blk.gpsimd
        def _(g):
            g.wait_ge(s_sp, cc_staged)
            if sim1:
                for p in range(W):
                    g.dma_start(out=ccout_d[p * DS:(p + 1) * DS, :],
                                in_=ccin_d[:]).then_inc(s_cc, 16)
            else:
                g.collective_compute(
                    "AllGather", OP.bypass,
                    replica_groups=[list(range(W))],
                    ins=[ccin_d.ap().opt()],
                    outs=[ccout_d.ap().opt()],
                ).then_inc(s_cc, 1)
            g.wait_ge(s_cc, 16 * W if sim1 else 1)
            tick(g.dma_start(out=GB[:], in_=ccout_d[0:DS, :]), "g", 16)
        g_gathered = cnt["g"]

        @blk.vector
        def _(v):
            v.wait_ge(s_g, g_gathered)
            tick(v.memset(ZSB[:], 0.0), "v")
        z_prev = ("v", cnt["v"])
        for p in range(W):
            @blk.vector
            def _(v):
                v.wait_ge(s_g, g_gathered + p * 16)
                v.wait_ge(SEMS[z_prev[0]], z_prev[1])
                v.scalar_tensor_tensor(PBL[:], GB[:, 0:DS],
                                       LAM[:, p:p + 1], LAMI[:, p * DS:(p + 1) * DS],
                                       OP.mult, OP.add)
                tick(v.tensor_scalar(CBL[:], GB[:, DS:DS + D],
                                     LAM[:, p:p + 1], None, OP.mult), "v")
            v_bl = cnt["v"]
            if p + 1 < W:
                @blk.gpsimd
                def _(g):
                    g.wait_ge(s_v, v_bl)
                    tick(g.dma_start(out=GB[:],
                                     in_=ccout_d[(p + 1) * DS:(p + 2) * DS, :]), "g", 16)
            @blk.tensor
            def _(t):
                t.wait_ge(s_v, v_bl)
                t.matmul(pC[0:DS, 0:512], PBL[:], ZSB[:, 0:512], start=True, stop=False)
                t.matmul(pC[0:DS, 0:512], I64H[:], CBL[:, 0:512], start=False, stop=True)
                t.matmul(pC[0:DS, 512:1024], PBL[:], ZSB[:, 512:1024], start=True, stop=False)
                tick(t.matmul(pC[0:DS, 512:1024], I64H[:], CBL[:, 512:1024],
                              start=False, stop=True), "t")
            t_z = cnt["t"]
            @blk.scalar
            def _(s):
                s.wait_ge(s_t, t_z)
                s.activation(ZSB[:, 0:512], pC[0:DS, 0:512], AF.Copy, bias=0.0, scale=1.0)
                tick(s.activation(ZSB[:, 512:1024], pC[0:DS, 512:1024], AF.Copy,
                                  bias=0.0, scale=1.0), "s")
            z_prev = ("s", cnt["s"])

        def zslot(tt, ch):
            if ch == 0:
                return ZSB if tt % 2 == 0 else ZC1
            return ZH[:, (ch - 1) * D: ch * D]

        ct_consumed = [0, 0]
        zstate = {"prev": z_prev}

        def chain_stages_p2(tt, ch):
            cidx = tt * NCH + ch
            slot = (cidx % 2) * D
            rckey = "rc0" if cidx % 2 == 0 else "rc1"
            src = zslot(tt, ch)
            dst = zslot(tt, ch + 1) if ch + 1 < NCH else zslot(tt + 1, 0)
            st = {}
            zp = zstate["prev"]
            @blk.sync
            def _(sy):
                sy.wait_ge(s_t, ct_consumed[cidx % 2])
                if tt == 0 and ch == 0:
                    sy.wait_ge(SEMS[zp[0]], zp[1])
                tick(sy.dma_start(out=CTL[:, slot:slot + D], in_=spct_d[cidx]), rckey, 16)
            r_ct = (rckey, cnt[rckey])
            yield
            @blk.tensor
            def _(t):
                t.wait_ge(SEMS[zp[0]], zp[1])
                t.wait_ge(SEMS[r_ct[0]], r_ct[1])
                pcs = PC[:, cidx * DS:(cidx + 1) * DS]
                t.matmul(pC[0:DS, 0:512], pcs, src[:, 0:512], start=True, stop=False)
                t.matmul(pC[0:DS, 0:512], I64H[:], CTL[:, slot:slot + 512],
                         start=False, stop=True)
                t.matmul(pC[0:DS, 512:1024], pcs, src[:, 512:1024], start=True, stop=False)
                tick(t.matmul(pC[0:DS, 512:1024], I64H[:], CTL[:, slot + 512:slot + D],
                              start=False, stop=True), "t")
            ct_consumed[cidx % 2] = cnt["t"]
            st["t_zc"] = cnt["t"]
            yield
            @blk.scalar
            def _(s):
                s.wait_ge(s_t, st["t_zc"])
                s.activation(dst[:, 0:512], pC[0:DS, 0:512], AF.Copy, bias=0.0, scale=1.0)
                tick(s.activation(dst[:, 512:1024], pC[0:DS, 512:1024], AF.Copy,
                                  bias=0.0, scale=1.0), "s")
            zstate["prev"] = ("s", cnt["s"])


        def emit_ycorr(tt, r_loaded, prev_ffn_v):
            marks = {}
            t_pd_free = cnt["t"]
            for ch in range(NCH):
                csl = [slice(kd * TT + ch * CH, kd * TT + (ch + 1) * CH)
                       for kd in range(KD)]
                src = zslot(tt, ch)
                for wave in range(2):
                    @blk.tensor
                    def _(t):
                        if ch == 0 and wave == 0:
                            t.wait_ge(s_r, r_loaded)
                            t.wait_ge(SEMS[zstate["prev"][0]], zstate["prev"][1])
                            t.wait_ge(s_v, cnt["v"])
                        else:
                            t.wait_ge(s_v, marks["x2"])
                        for j in range(4):
                            kd = wave * 4 + j
                            i = t.matmul(pD[:, j * 128:(j + 1) * 128],
                                         src[:, kd * 128:(kd + 1) * 128],
                                         QHL[:, ch * CH:(ch + 1) * CH],
                                         start=True, stop=True)
                        tick(i, "t")
                    t_y = cnt["t"]
                    @blk.vector
                    def _(v):
                        v.wait_ge(s_t, t_y)
                        if ch == 0 and wave == 0:
                            v.wait_ge(s_v, prev_ffn_v)
                            v.wait_ge(s_sp, rp_tile_spill[NTILE - 1])
                        for j in range(4):
                            kd = wave * 4 + j
                            i = v.tensor_tensor(RP[:, csl[kd]], HA[:, csl[kd]],
                                                pD[:, j * 128:(j + 1) * 128], OP.add)
                        tick(i, "v")
                    marks["x2"] = cnt["v"]
            return marks["x2"]

        prev_ffn2_v = [0]
        prev_out = [0]
        p2_reload = [0] * NTILE

        def emit_reload(tt):
            @blk.sync
            def _(sy):
                sy.wait_ge(s_sp, rp_tile_spill[tt])
                sy.wait_ge(s_r, cnt["r"])
                if tt > 0:
                    sy.wait_ge(s_v, p2_reload_guard[0])
                else:
                    sy.wait_ge(s_t, cnt["t"])
                    sy.wait_ge(s_v, cnt["v"])
                for kd in range(KD):
                    tick(sy.dma_start(out=HA[:, kd * TT:(kd + 1) * TT],
                                      in_=sprp_d[kd, :, tt * TT:(tt + 1) * TT]), "r", 16)
                for ch in range(NCH):
                    tick(sy.dma_start(out=QHL[:, ch * CH:(ch + 1) * CH],
                                      in_=spqh_d[tt * NCH + ch]), "r", 16)
            p2_reload[tt] = cnt["r"]

        p2_reload_guard = [0]
        emit_reload(0)
        for ch in range(NCH):
            pending.append(chain_stages_p2(0, ch))
        pump_all()

        for tt in range(NTILE):
            x2_v = emit_ycorr(tt, p2_reload[tt], prev_ffn2_v[0])
            p2_reload_guard[0] = x2_v
            st3 = emit_ln_stats([("v", x2_v)], RP, pump=pump)
            c0 = emit_ln_tail("q", 0, st3)
            if tt + 1 < NTILE:
                emit_reload(tt + 1)
                for ch in range(NCH):
                    pending.append(chain_stages_p2(tt + 1, ch))
            emit_ffn_half(tt, 0, c0, RP, False,
                          evac_waits=[("o", prev_out[0])], pump=pump)
            c1 = emit_ln_tail("q", 1, st3)
            ffn2_v = emit_ffn_half(tt, 1, c1, RP, False,
                                   evac_waits=[("o", prev_out[0])], pump=pump)
            pump_all()
            prev_ffn2_v[0] = ffn2_v


            @blk.sync
            def _(sy):
                sy.wait_ge(s_v, ffn2_v)
                for kd in range(KD):
                    tick(sy.dma_start(out=out_d[kd, :, tt * TT:(tt + 1) * TT],
                                      in_=XF[:, kd * TT:(kd + 1) * TT]), "o", 16)
            prev_out[0] = cnt["o"]

        @blk.sync
        def _(sy):
            sy.wait_ge(s_o, cnt["o"])
    return nc


def threefry2x32_np(k0, k1, c0, c1):
    ks0, ks1 = np.uint32(k0), np.uint32(k1)
    ks2 = np.uint32(ks0 ^ ks1 ^ np.uint32(0x1BD11BDA))
    rot = [[13, 15, 26, 6], [17, 29, 16, 24]]
    def rotl(x, d):
        return ((x << np.uint32(d)) | (x >> np.uint32(32 - d))).astype(np.uint32)
    x0 = (c0 + ks0).astype(np.uint32); x1 = (c1 + ks1).astype(np.uint32)
    ks = [ks0, ks1, ks2]
    for i in range(5):
        r = rot[0] if i % 2 == 0 else rot[1]
        for j in range(4):
            x0 = (x0 + x1).astype(np.uint32)
            x1 = rotl(x1, r[j])
            x1 = (x1 ^ x0).astype(np.uint32)
        x0 = (x0 + ks[(i + 1) % 3]).astype(np.uint32)
        x1 = (x1 + ks[(i + 2) % 3] + np.uint32(i + 1)).astype(np.uint32)
    return x0, x1


def host_prep(inp):
    x = np.ascontiguousarray(np.asarray(inp["L0_attn_x"], np.float32))
    B, S, Dm = x.shape
    xf = x.reshape(B * S, Dm)
    n = B * S * Dm
    c2 = np.arange(n, dtype=np.uint32)
    c1 = np.zeros(n, np.uint32)
    a0, a1 = threefry2x32_np(0, 42, c1, c2)
    bits = (a0 ^ a1).astype(np.uint32)
    f = ((bits >> np.uint32(9)) | np.uint32(0x3F800000)).view(np.float32) - np.float32(1.0)
    keep = np.maximum(np.float32(0.0), f) < np.float32(0.8)
    mask = np.where(keep.reshape(B * S, Dm), np.float16(0.625), np.float16(0.0))

    w1 = np.asarray(inp["L0_ffn_w1"], np.float32)
    w2 = np.asarray(inp["L0_ffn_w2"], np.float32)
    wv = np.asarray(inp["L0_attn_Wv"], np.float32)
    wk = np.asarray(inp["L0_attn_Wk"], np.float32)
    wq = np.asarray(inp["L0_attn_Wq"], np.float32)
    w1t = np.ascontiguousarray(w1.T.reshape(KD, 128, Hh))
    w2t = np.ascontiguousarray(w2.T.reshape(KH, 128, D))
    wvt = np.ascontiguousarray(wv.T.reshape(KD, 128, D))
    wkt = np.ascontiguousarray(wk.T.reshape(KD, 128, DS))
    wqt = np.ascontiguousarray(wq.T.reshape(KD, 128, DS))
    bias_h = np.ascontiguousarray(np.asarray(inp["L0_ffn_b1"], np.float32)
                                  .reshape(KH, 128).T)
    vecd = np.zeros((128, 24), np.float32)
    vecd[:, 0:8] = np.asarray(inp["L0_ffn_b2"], np.float32).reshape(KD, 128).T
    vecd[:, 8:16] = np.asarray(inp["L0_attn_gamma"], np.float32).reshape(KD, 128).T
    vecd[:, 16:24] = np.asarray(inp["L0_attn_beta"], np.float32).reshape(KD, 128).T
    bias_e = np.stack([np.asarray(inp["L0_attn_bk"], np.float32),
                       np.asarray(inp["L0_attn_bq"], np.float32)], axis=1)
    bvrow = np.asarray(inp["L0_attn_bv"], np.float32).reshape(1, D)
    bkrow = np.asarray(inp["L0_attn_bk"], np.float32).reshape(1, DS)
    consts = np.zeros((128, 576), np.float32)
    consts[:, 0:128] = np.tril(np.ones((128, 128), np.float32), -1)
    consts[:, 128:256] = np.triu(np.ones((128, 128), np.float32), 1)
    consts[:, 256:384] = 2.0 * np.eye(128, dtype=np.float32)
    consts[:, 384:512] = np.eye(128, dtype=np.float32)
    consts[0:DS, 512:576] = np.eye(DS, dtype=np.float32)
    i64h = np.eye(DS, dtype=np.float16)

    shared = dict(w1t=w1t, w2t=w2t, wvt=wvt, wkt=wkt, wqt=wqt, bias_h=bias_h,
                  vecd=vecd, bias_e=bias_e, bvrow=bvrow, bkrow=bkrow,
                  consts=consts, i64h=i64h)
    in_maps = []
    spanlen = B * S // W
    for c in range(W):
        sl = slice(c * spanlen, (c + 1) * spanlen)
        xs = np.ascontiguousarray(xf[sl].T.reshape(KD, 128, SPAN))
        ms = np.ascontiguousarray(mask[sl].T.reshape(KD, 128, SPAN))
        lam = np.zeros((DS, W), np.float32)
        g0 = 4 * (c // 4)
        for p in range(g0, c):
            lam[:, p] = 1.0
        lamI = np.zeros((W, DS, DS), np.float16)
        for p in range(W):
            lamI[p] = np.eye(DS, dtype=np.float16) * (0.0 if g0 <= p < c else 1.0)
        lamI = np.ascontiguousarray(lamI.transpose(1, 0, 2).reshape(DS, W * DS))
        m = dict(shared)
        m["x"] = xs
        m["mask"] = ms
        m["lam"] = lam
        m["lamI"] = lamI
        in_maps.append(m)
    return in_maps


def gather_out(results):
    outs = []
    for c in range(W):
        o = results[c]["out"]
        outs.append(np.ascontiguousarray(o.reshape(D, SPAN).T))
    return np.concatenate(outs, axis=0)


def kernel(**inputs):
    import concourse.bass_utils as bass_utils
    nc = build_program("full")
    in_maps = host_prep(inputs)
    res = bass_utils.run_bass_kernel_spmd(nc, in_maps, core_ids=list(range(W)))
    out = gather_out(res.results)
    return out.reshape(2, 16384, 1024)


# version 07
# speedup vs baseline: 1.1067x; 1.1067x over previous
import numpy as np
from contextlib import ExitStack
import concourse.bass as bass
import concourse.mybir as mybir

F32 = mybir.dt.float32
F16 = mybir.dt.float16
F8 = mybir.dt.float8e4
AF = mybir.ActivationFunctionType
OP = mybir.AluOpType

D = 1024
Hh = 4096
DS = 64
SPAN = 4096
TT = 512
NTILE = SPAN // TT
CH = 128
NCH = TT // CH
KD = D // 128
KH = Hh // 128
LR = float(np.float32(0.01))
INV_D = float(np.float32(1.0) / np.float32(1024.0))
EPS = float(np.float32(1e-5))
W = 8
HTT = TT // 2


def build_program(stage="full", sim1=False):
    nc = bass.Bass("TRN2", target_bir_lowering=False)
    x_d = nc.dram_tensor("x", [KD, 128, SPAN], F32, kind="ExternalInput")
    mk_d = nc.dram_tensor("mask", [KD, 128, SPAN], F16, kind="ExternalInput")
    w1_d = nc.dram_tensor("w1t", [KD, 128, Hh], F32, kind="ExternalInput")
    w2_d = nc.dram_tensor("w2t", [KH, 128, D], F32, kind="ExternalInput")
    wv_d = nc.dram_tensor("wvt", [KD, 128, D], F32, kind="ExternalInput")
    wk_d = nc.dram_tensor("wkt", [KD, 128, DS], F32, kind="ExternalInput")
    wq_d = nc.dram_tensor("wqt", [KD, 128, DS], F32, kind="ExternalInput")
    bh_d = nc.dram_tensor("bias_h", [128, KH], F32, kind="ExternalInput")
    vd_d = nc.dram_tensor("vecd", [128, 24], F32, kind="ExternalInput")
    be_d = nc.dram_tensor("bias_e", [DS, 2], F32, kind="ExternalInput")
    bv_d = nc.dram_tensor("bvrow", [1, D], F32, kind="ExternalInput")
    bk_d = nc.dram_tensor("bkrow", [1, DS], F32, kind="ExternalInput")
    cst_d = nc.dram_tensor("consts", [128, 576], F32, kind="ExternalInput")
    i64_d = nc.dram_tensor("i64h", [DS, DS], F16, kind="ExternalInput")
    lam_d = nc.dram_tensor("lam", [DS, W], F32, kind="ExternalInput")
    lamI_d = nc.dram_tensor("lamI", [DS, W * DS], F16, kind="ExternalInput")
    out_d = nc.dram_tensor("out", [KD, 128, SPAN], F32, kind="ExternalOutput")

    sprp_d = nc.dram_tensor("sp_rp", [KD, 128, SPAN], F16)
    spqh_d = nc.dram_tensor("sp_qh", [NTILE * NCH, DS, CH], F16)
    spct_d = nc.dram_tensor("sp_ct", [NTILE * NCH, DS, D], F16)
    ccin_d = nc.dram_tensor("cc_in", [DS, DS + D], F16)
    ccout_d = nc.dram_tensor("cc_out", [W * DS, DS + D], F16)

    es = ExitStack()
    def sb(name, shape, dt):
        return es.enter_context(nc.sbuf_tensor(name, shape, dt))
    def psum(name, shape, dt=F32):
        return es.enter_context(nc.psum_tensor(name, shape, dt))
    def sem(name):
        return es.enter_context(nc.semaphore(name))

    with es:
        W1Q = sb("W1Q", [128, KD * Hh], F8)
        W2Q = sb("W2Q", [128, KH * D], F8)
        WVT = sb("WVT", [128, KD * D], F16)
        WKT = sb("WKT", [128, KD * DS], F16)
        WQT = sb("WQT", [128, KD * DS], F16)
        B1Q = sb("B1Q", [128, KH], F32)
        B8 = sb("B8", [128, KH], F8)
        B2Q = sb("B2Q", [128, KD], F32)
        B28 = sb("B28", [128, KD], F8)
        GAM = sb("GAM", [128, KD], F32)
        BET = sb("BET", [128, KD], F32)
        BKE = sb("BKE", [DS, 2], F32)
        BVR = sb("BVR", [1, D], F16)
        BKR = sb("BKR", [1, DS], F16)
        ML = sb("ML", [128, 128], F16)
        MU = sb("MU", [128, 128], F16)
        TWOI = sb("TWOI", [128, 128], F16)
        I128F = sb("I128F", [128, 128], F16)
        I64F = sb("I64F", [DS, DS], F32)
        I64H = sb("I64H", [DS, DS], F16)
        LAM = sb("LAM", [DS, W], F32)
        LAMI = sb("LAMI", [DS, W * DS], F16)
        ONE16 = sb("ONE16", [128, 1], F16)
        ONE1F = sb("ONE1F", [1, 128], F32)
        ONE1H = sb("ONE1H", [1, 128], F16)
        EPSA = sb("EPSA", [1, 1], F32)
        ROWS = sb("ROWS", [1, 4 * TT], F32)
        XF = sb("XF", [128, KD * TT], F32)
        XB = sb("XB", [128, KD * TT], F16)
        MKS = sb("MKS", [128, 2 * TT], F16)
        LQ = sb("LQ", [128, KD * TT], F8)
        H8 = sb("H8", [128, KH * TT], F8)
        H16T = sb("H16T", [128, 2 * TT], F16)
        Y16T = sb("Y16T", [128, 2 * TT], F16)
        F32T = sb("F32T", [128, 2 * TT], F32)
        TMPA = sb("TMPA", [128, 2 * CH], F16)
        TMPB = sb("TMPB", [128, 2 * CH], F16)
        TMPF = sb("TMPF", [128, 2 * HTT], F32)
        RBF = sb("RBF", [128, 2 * HTT], F32)
        RB16 = sb("RB16", [128, 2], F16)
        HA = sb("HA", [128, KD * TT], F16)
        RP = sb("RP", [128, KD * TT], F16)
        VSB = sb("VSB", [128, 2 * D], F16)
        KSB = sb("KSB", [DS, 2 * CH], F16)
        QSB = sb("QSB", [DS, 2 * CH], F16)
        KTM = sb("KTM", [128, 2 * DS], F16)
        TB = sb("TB", [128, 2 * 128], F16)
        MSB = sb("MSB", [128, 2 * 128], F16)
        MTSB = sb("MTSB", [128, 2 * 128], F16)
        HB = sb("HB", [128, 2 * 128], F16)
        WTT = sb("WTT", [128, 2 * 128], F16)
        UV = sb("UV", [128, 2 * D], F16)
        UK = sb("UK", [128, 2 * DS], F16)
        QH = sb("QH", [DS, CH], F16)
        PC = sb("PC", [DS, NTILE * NCH * DS], F16)
        CTC = sb("CTC", [DS, D], F16)
        SPB = sb("SPB", [DS, DS], F16)
        SPTB = sb("SPTB", [DS, DS], F16)
        SCTB = sb("SCTB", [DS, D], F16)
        ZSB = sb("ZSB", [DS, D], F16)
        ZC1 = sb("ZC1", [DS, D], F16)
        ZH = sb("ZH", [DS, (NCH - 1) * D], F16)
        QHL = sb("QHL", [DS, NCH * CH], F16)
        CTL = sb("CTL", [DS, 2 * D], F16)
        GB = CTL[0:DS, 0:DS + D]
        PBL = QH[0:DS, 0:DS]
        CBL = CTC

        pA = [psum("pA0", [128, TT]), psum("pA1", [128, TT])]
        pB = [psum("pB0", [128, TT]), psum("pB1", [128, TT])]
        pC = psum("pC", [128, D])
        pD = psum("pD", [128, 512])
        pE = psum("pE", [128, 512])
        pE2A = pD[:, 384:512]

        s_w = sem("s_w"); s_wg = sem("s_wg"); s_x = sem("s_x")
        s_sp = sem("s_sp"); s_r = sem("s_r")
        s_rc0 = sem("s_rc0"); s_rc1 = sem("s_rc1")
        s_mk0 = sem("s_mk0"); s_mk1 = sem("s_mk1")
        s_mk2 = sem("s_mk2"); s_mk3 = sem("s_mk3")
        s_o = sem("s_o"); s_cc = sem("s_cc")
        s_t = sem("s_t"); s_v = sem("s_v"); s_s = sem("s_s"); s_g = sem("s_g")

        cnt = {"t": 0, "v": 0, "s": 0, "g": 0, "w": 0, "wg": 0, "x": 0, "sp": 0,
               "r": 0, "o": 0, "rc0": 0, "rc1": 0, "mk0": 0, "mk1": 0,
               "mk2": 0, "mk3": 0}
        SEMS = {"t": s_t, "v": s_v, "s": s_s, "g": s_g, "w": s_w, "wg": s_wg,
                "x": s_x, "sp": s_sp, "r": s_r, "o": s_o, "rc0": s_rc0, "rc1": s_rc1,
                "mk0": s_mk0, "mk1": s_mk1, "mk2": s_mk2, "mk3": s_mk3}

        def tick(instr, key, n=1):
            cnt[key] += n
            instr.then_inc(SEMS[key], n)
            return cnt[key]

        blk = es.enter_context(nc.Block())

        @blk.gpsimd
        def _(g):
            for kd in range(KD):
                tick(g.dma_start(out=W1Q[:, kd * Hh:(kd + 1) * Hh], in_=w1_d[kd]), "wg", 16)
            for kh in range(KH):
                if kh == 16:
                    g.wait_ge(s_wg, cnt["wg"])
                tick(g.dma_start(out=W2Q[:, kh * D:(kh + 1) * D], in_=w2_d[kh]), "wg", 16)
            g.wait_ge(s_wg, cnt["wg"])
            for kd in range(KD):
                tick(g.dma_start(out=WVT[:, kd * D:(kd + 1) * D], in_=wv_d[kd]), "wg", 16)
            for kd in range(KD):
                tick(g.dma_start(out=WKT[:, kd * DS:(kd + 1) * DS], in_=wk_d[kd]), "wg", 16)
                tick(g.dma_start(out=WQT[:, kd * DS:(kd + 1) * DS], in_=wq_d[kd]), "wg", 16)
            tick(g.dma_start(out=B8[:], in_=bh_d[:]), "wg", 16)
            tick(g.dma_start(out=B28[:], in_=vd_d[:, 0:KD]), "wg", 16)
            tick(g.dma_start(out=BVR[:], in_=bv_d[:]), "wg", 16)
            tick(g.dma_start(out=BKR[:], in_=bk_d[:]), "wg", 16)
            tick(g.dma_start(out=LAMI[:], in_=lamI_d[:]), "wg", 16)
            tick(g.dma_start(out=ML[:], in_=cst_d[:, 0:128]), "wg", 16)
            tick(g.dma_start(out=MU[:], in_=cst_d[:, 128:256]), "wg", 16)
            tick(g.dma_start(out=TWOI[:], in_=cst_d[:, 256:384]), "wg", 16)
            tick(g.dma_start(out=I128F[:], in_=cst_d[:, 384:512]), "wg", 16)

        @blk.sync
        def _(sy):
            tick(sy.dma_start(out=GAM[:], in_=vd_d[:, 8:16]), "w", 16)
            tick(sy.dma_start(out=BET[:], in_=vd_d[:, 16:24]), "w", 16)
            tick(sy.dma_start(out=BKE[:], in_=be_d[:]), "w", 16)
            tick(sy.dma_start(out=I64F[:], in_=cst_d[0:DS, 512:576]), "w", 16)
            tick(sy.dma_start(out=I64H[:], in_=i64_d[:]), "w", 16)
            tick(sy.dma_start(out=LAM[:], in_=lam_d[:]), "w", 16)
        w_done = cnt["w"]
        wg_done = cnt["wg"]

        @blk.scalar
        def _(s):
            s.preload_act_table(AF.Sqrt)

        @blk.vector
        def _(v):
            v.wait_ge(s_w, w_done)
            v.wait_ge(s_wg, wg_done)
            v.memset(ONE16[:], 1.0)
            v.memset(RB16[:], 0.0)
            v.memset(ONE1F[:], 1.0)
            v.memset(ONE1H[:], 1.0)
            v.memset(EPSA[:], EPS)
            v.tensor_copy(B1Q[:], B8[:])
            v.tensor_copy(B2Q[:], B28[:])
            v.tensor_copy(SPB[:], I64H[:])
            v.tensor_copy(SPTB[:], I64H[:])
            tick(v.memset(SCTB[:], 0.0), "v")
        init_v = cnt["v"]

        rel = {"LQ0": ("t", 0), "LQ1": ("t", 0), "XB": ("t", 0),
               "H80": ("t", 0), "H81": ("t", 0)}

        def emit_ln_stats(waits, src, pump=None):
            xb_rel = rel["XB"]
            @blk.vector
            def _(v):
                for k, val in waits:
                    v.wait_ge(SEMS[k], val)
                v.wait_ge(SEMS[xb_rel[0]], xb_rel[1])
                for kd in range(KD):
                    i = v.tensor_scalar(XB[:, kd * TT:(kd + 1) * TT],
                                        src[:, kd * TT:(kd + 1) * TT],
                                        float(1.0 / 64.0), None, OP.mult)
                tick(i, "v")
            v_copy = cnt["v"]
            if pump: pump()

            s_prior = cnt["s"]
            @blk.tensor
            def _(t):
                t.wait_ge(s_v, v_copy)
                t.wait_ge(s_s, s_prior)
                for kd in range(KD):
                    i = t.matmul(pE[0:1, 0:TT], ONE16[:], XB[:, kd * TT:(kd + 1) * TT],
                                 start=(kd == 0), stop=(kd == KD - 1))
                tick(i, "t")
            rel["XB"] = ("t", cnt["t"])
            mmtick = [0]
            for p in range(KD // 2):
                @blk.vector
                def _(v):
                    if p == 0:
                        v.wait_ge(s_v, v_copy)
                    else:
                        v.wait_ge(s_t, mmtick[0])
                    v.tensor_tensor(H16T[:, 0:TT], XB[:, (2 * p) * TT:(2 * p + 1) * TT],
                                    XB[:, (2 * p) * TT:(2 * p + 1) * TT], OP.mult)
                    tick(v.tensor_tensor(H16T[:, TT:2 * TT],
                                         XB[:, (2 * p + 1) * TT:(2 * p + 2) * TT],
                                         XB[:, (2 * p + 1) * TT:(2 * p + 2) * TT],
                                         OP.mult), "v")
                v_sqp = cnt["v"]
                @blk.tensor
                def _(t):
                    t.wait_ge(s_v, v_sqp)
                    t.matmul(pE[32:33, 0:TT], ONE16[:], H16T[:, 0:TT],
                             start=(p == 0), stop=False)
                    tick(t.matmul(pE[32:33, 0:TT], ONE16[:], H16T[:, TT:2 * TT],
                                  start=False, stop=(p == KD // 2 - 1)), "t")
                mmtick[0] = cnt["t"]
            t_sq = cnt["t"]
            if pump: pump()

            @blk.vector
            def _(v):
                v.wait_ge(s_t, t_sq)
                tick(v.tensor_scalar(ROWS[0:1, 0:TT], pE[0:1, 0:TT],
                                     float(64.0 / 1024.0), None, OP.mult), "v")
                v.wait_ge(s_v, cnt["v"])
                tick(v.tensor_tensor(ROWS[0:1, 2 * TT:3 * TT], ROWS[0:1, 0:TT],
                                     ROWS[0:1, 0:TT], OP.mult), "v")
                v.wait_ge(s_v, cnt["v"])
                tick(v.scalar_tensor_tensor(ROWS[0:1, TT:2 * TT], pE[32:33, 0:TT], 4.0,
                                            ROWS[0:1, 2 * TT:3 * TT],
                                            OP.mult, OP.subtract), "v")
            v_var = cnt["v"]

            @blk.scalar
            def _(s):
                s.wait_ge(s_v, v_var)
                tick(s.activation(ROWS[0:1, TT:2 * TT], ROWS[0:1, TT:2 * TT],
                                  AF.Sqrt, bias=EPSA[:, 0:1]), "s")
            s_sd = cnt["s"]

            @blk.vector
            def _(v):
                v.wait_ge(s_s, s_sd)
                tick(v.reciprocal(ROWS[0:1, 2 * TT:3 * TT], ROWS[0:1, TT:2 * TT]), "v")
                v.wait_ge(s_v, cnt["v"])
                tick(v.tensor_tensor(ROWS[0:1, 3 * TT:4 * TT], ROWS[0:1, 0:TT],
                                     ROWS[0:1, 2 * TT:3 * TT], OP.mult), "v")
            v_rows = cnt["v"]
            if pump: pump()
            return {"v_rows": v_rows, "src": src, "s_bc0": None, "v_ap0": None}

        def emit_ln_tail(out_kind, hh, ST, apply_waits=()):
            src = ST["src"]
            lq_rel = rel["LQ%d" % hh]
            @blk.tensor
            def _(t):
                if hh == 0:
                    t.wait_ge(s_v, ST["v_rows"])
                    t.wait_ge(s_s, cnt["s"])
                else:
                    t.wait_ge(s_s, ST["s_bc0"])
                t.matmul(pE[:, 0:HTT], ONE1F[:],
                         ROWS[0:1, 2 * TT + hh * HTT: 2 * TT + (hh + 1) * HTT],
                         start=True, stop=True)
                tick(t.matmul(pE[:, HTT:2 * HTT], ONE1F[:],
                              ROWS[0:1, 3 * TT + hh * HTT: 3 * TT + (hh + 1) * HTT],
                              start=True, stop=True), "t")
            t_bc = cnt["t"]

            @blk.scalar
            def _(s):
                s.wait_ge(s_t, t_bc)
                if hh == 1:
                    s.wait_ge(s_v, ST["v_ap0"])
                s.activation(RBF[:, 0:HTT], pE[:, 0:HTT], AF.Copy,
                             bias=0.0, scale=1.0)
                tick(s.activation(RBF[:, HTT:2 * HTT], pE[:, HTT:2 * HTT],
                                  AF.Copy, bias=0.0, scale=1.0), "s")
            s_bc = cnt["s"]
            if hh == 0:
                ST["s_bc0"] = s_bc

            @blk.vector
            def _(v):
                v.wait_ge(s_s, s_bc)
                if out_kind == "q":
                    v.wait_ge(SEMS[lq_rel[0]], lq_rel[1])
                for k, val in apply_waits:
                    v.wait_ge(SEMS[k], val)
                rstd = RBF[:, 0:HTT]
                mr = RBF[:, HTT:2 * HTT]
                tf = [TMPF[:, 0:HTT], TMPF[:, HTT:2 * HTT]]
                for kp in range(KD // 2):
                    kk = [2 * kp, 2 * kp + 1]
                    sls = [slice(kk[i] * TT + hh * HTT,
                                 kk[i] * TT + (hh + 1) * HTT) for i in (0, 1)]
                    if kp > 0 or hh == 1:
                        v.wait_ge(s_v, cnt["v"])
                    v.tensor_tensor(tf[0], src[:, sls[0]], rstd, OP.mult)
                    tick(v.tensor_tensor(tf[1], src[:, sls[1]], rstd, OP.mult), "v")
                    v.wait_ge(s_v, cnt["v"])
                    if out_kind == "q":
                        v.tensor_tensor(LQ[:, sls[0]], tf[0], mr, OP.subtract)
                        tick(v.tensor_tensor(LQ[:, sls[1]], tf[1], mr,
                                             OP.subtract), "v")
                    else:
                        v.tensor_tensor(tf[0], tf[0], mr, OP.subtract)
                        tick(v.tensor_tensor(tf[1], tf[1], mr, OP.subtract), "v")
                        v.wait_ge(s_v, cnt["v"])
                        for i in (0, 1):
                            v.tensor_tensor(RP[:, sls[i]], src[:, sls[i]], tf[i],
                                            OP.add)
                        v.tensor_scalar(HA[:, sls[0]], tf[0],
                                        GAM[:, kk[0]:kk[0] + 1],
                                        BET[:, kk[0]:kk[0] + 1], OP.mult, OP.add)
                        tick(v.tensor_scalar(HA[:, sls[1]], tf[1],
                                             GAM[:, kk[1]:kk[1] + 1],
                                             BET[:, kk[1]:kk[1] + 1],
                                             OP.mult, OP.add), "v")
            v_apply = cnt["v"]
            if hh == 0:
                ST["v_ap0"] = v_apply
            return v_apply

        mk_guard = {0: ("v", 0), 1: ("v", 0)}
        def emit_ffn(tt, ln_v, res_src, res_inplace, evac_waits=(), pump=None):
            s_preh = cnt["s"]
            h8_rel = rel["H80"]
            tm = {}; sm = {}; vm = {}
            for kh in range(KH):
                @blk.tensor
                def _(t):
                    if kh == 0:
                        t.wait_ge(s_v, ln_v)
                    if kh < 2:
                        t.wait_ge(s_s, s_preh)
                    else:
                        t.wait_ge(s_s, sm[kh - 2])
                    bank = pA[kh % 2]
                    for kd in range(KD):
                        i = t.matmul(bank[:, 0:TT],
                                     W1Q[:, kd * Hh + kh * 128: kd * Hh + (kh + 1) * 128],
                                     LQ[:, kd * TT:(kd + 1) * TT],
                                     start=(kd == 0), stop=(kd == KD - 1))
                    tick(i, "t")
                tm[kh] = cnt["t"]
                @blk.scalar
                def _(s):
                    s.wait_ge(s_t, tm[kh])
                    if kh >= 2:
                        s.wait_ge(s_v, vm[kh - 2])
                    tick(s.activation(H16T[:, (kh % 2) * TT:(kh % 2 + 1) * TT],
                                      pA[kh % 2][:, 0:TT], AF.Relu,
                                      bias=B1Q[:, kh:kh + 1], scale=1.0), "s")
                sm[kh] = cnt["s"]
                @blk.vector
                def _(v):
                    if kh == 0:
                        v.wait_ge(SEMS[h8_rel[0]], h8_rel[1])
                    v.wait_ge(s_s, sm[kh])
                    tick(v.tensor_copy(H8[:, kh * TT:(kh + 1) * TT],
                                       H16T[:, (kh % 2) * TT:(kh % 2 + 1) * TT]), "v")
                vm[kh] = cnt["v"]
                if pump:
                    pump()
            rel["LQ0"] = rel["LQ1"] = ("t", tm[KH - 1])

            tm2 = {}; sm2 = {}; vm2 = {}
            def emit_add(v, pd):
                sl = slice(pd * TT, (pd + 1) * TT)
                pft = F32T[:, (pd % 2) * TT:(pd % 2 + 1) * TT]
                if res_inplace:
                    return v.tensor_tensor(XF[:, sl], XF[:, sl], pft, OP.add)
                return v.tensor_tensor(XF[:, sl], res_src[:, sl], pft, OP.add)
            for dv in range(KD):
                p2 = dv % 2
                mkkey = "mk%d" % p2
                g0 = mk_guard[p2]
                @blk.sync
                def _(sy):
                    sy.wait_ge(SEMS[g0[0]], g0[1])
                    tick(sy.dma_start(out=MKS[:, p2 * TT:(p2 + 1) * TT],
                                      in_=mk_d[dv, :, tt * TT:(tt + 1) * TT]), mkkey, 16)
                mk_ld = cnt[mkkey]
                @blk.tensor
                def _(t):
                    if dv == 0:
                        t.wait_ge(s_v, vm[KH - 1])
                    if dv < 2:
                        t.wait_ge(s_s, s_preh)
                    else:
                        t.wait_ge(s_s, sm2[dv - 2])
                    bank = pB[p2]
                    for kh in range(KH):
                        i = t.matmul(bank[:, 0:TT],
                                     W2Q[:, kh * D + dv * 128: kh * D + (dv + 1) * 128],
                                     H8[:, kh * TT:(kh + 1) * TT],
                                     start=(kh == 0), stop=(kh == KH - 1))
                    tick(i, "t")
                tm2[dv] = cnt["t"]
                @blk.scalar
                def _(s):
                    s.wait_ge(s_t, tm2[dv])
                    if dv >= 2:
                        s.wait_ge(s_v, vm2[dv - 2])
                    tick(s.activation(Y16T[:, p2 * TT:(p2 + 1) * TT],
                                      pB[p2][:, 0:TT], AF.Identity,
                                      bias=B2Q[:, dv:dv + 1], scale=1.0), "s")
                sm2[dv] = cnt["s"]
                @blk.vector
                def _(v):
                    if dv == 0:
                        for k, val in evac_waits:
                            v.wait_ge(SEMS[k], val)
                    v.wait_ge(s_s, sm2[dv])
                    v.wait_ge(SEMS[mkkey], mk_ld)
                    if dv > 0:
                        v.wait_ge(s_v, vm2[dv - 1])
                        emit_add(v, dv - 1)
                    tick(v.tensor_tensor(F32T[:, p2 * TT:(p2 + 1) * TT],
                                         Y16T[:, p2 * TT:(p2 + 1) * TT],
                                         MKS[:, p2 * TT:(p2 + 1) * TT],
                                         OP.mult), "v")
                vm2[dv] = cnt["v"]
                mk_guard[p2] = ("v", cnt["v"])
                if pump:
                    pump()
            @blk.vector
            def _(v):
                v.wait_ge(s_v, vm2[KD - 1])
                tick(emit_add(v, KD - 1), "v")
            rel["H80"] = rel["H81"] = ("t", tm2[KD - 1])
            return cnt["v"]

        qhct_prev = [0]
        rp_prev = [0]
        span_s = [0]
        rp_tile_spill = [0] * NTILE
        vproj_last = [("s", 0)]

        def pair_stages_p1(tt, pidx, ha_ready_v):
            PX = [0, 1]
            cidxs = [tt * NCH + 2 * pidx, tt * NCH + 2 * pidx + 1]
            chs = [2 * pidx, 2 * pidx + 1]
            csls = [[slice(kd * TT + ch * CH, kd * TT + (ch + 1) * CH)
                     for kd in range(KD)] for ch in chs]
            st = {}
            for j in (0, 1):
                par = PX[j]
                @blk.tensor
                def _(t):
                    if j == 0:
                        t.wait_ge(s_v, ha_ready_v)
                    w = vproj_last[0]
                    t.wait_ge(SEMS[w[0]], w[1])
                    if span_s[0]:
                        t.wait_ge(s_s, span_s[0])
                    for half in range(2):
                        for kd in range(KD):
                            t.matmul(pC[0:CH, half * 512:(half + 1) * 512],
                                     HA[:, csls[j][kd]],
                                     WVT[:, kd * D + half * 512: kd * D + (half + 1) * 512],
                                     start=(kd == 0), stop=False)
                        i = t.matmul(pC[0:CH, half * 512:(half + 1) * 512], ONE1H[:],
                                     BVR[0:1, half * 512:(half + 1) * 512],
                                     start=False, stop=True)
                    for kd in range(KD):
                        t.matmul(pD[0:DS, 0:CH], WKT[:, kd * DS:(kd + 1) * DS],
                                 HA[:, csls[j][kd]], start=(kd == 0), stop=(kd == KD - 1))
                    for kd in range(KD):
                        t.matmul(pD[0:DS, CH:2 * CH], WQT[:, kd * DS:(kd + 1) * DS],
                                 HA[:, csls[j][kd]], start=(kd == 0), stop=(kd == KD - 1))
                    for kd in range(KD):
                        t.matmul(pD[0:CH, 320:320 + DS], HA[:, csls[j][kd]],
                                 WKT[:, kd * DS:(kd + 1) * DS], start=(kd == 0), stop=False)
                    tick(t.matmul(pD[0:CH, 320:320 + DS], ONE1H[:], BKR[:],
                                  start=False, stop=True), "t")
                st[("t_proj", j)] = cnt["t"]
                yield
                @blk.scalar
                def _(s):
                    s.wait_ge(s_t, st[("t_proj", j)])
                    s.activation(VSB[:, par * D:par * D + 512], pC[0:CH, 0:512],
                                 AF.Copy, bias=0.0, scale=1.0)
                    s.activation(VSB[:, par * D + 512:par * D + 1024],
                                 pC[0:CH, 512:1024], AF.Copy, bias=0.0, scale=1.0)
                    s.activation(KSB[:, par * CH:(par + 1) * CH], pD[0:DS, 0:CH],
                                 AF.Identity, bias=BKE[:, 0:1], scale=1.0)
                    s.activation(QSB[:, par * CH:(par + 1) * CH], pD[0:DS, CH:2 * CH],
                                 AF.Identity, bias=BKE[:, 1:2], scale=1.0)
                    tick(s.activation(KTM[:, par * DS:(par + 1) * DS],
                                      pD[0:CH, 320:320 + DS], AF.Copy,
                                      bias=0.0, scale=1.0), "s")
                st[("s_proj", j)] = cnt["s"]
                vproj_last[0] = ("s", cnt["s"])
                yield
            @blk.tensor
            def _(t):
                t.wait_ge(s_s, st[("s_proj", 1)])
                t.matmul(pD[:, 256:384], KSB[:, 0:CH], KSB[:, 0:CH], start=True, stop=True)
                tick(t.matmul(pD[:, 384:512], KSB[:, CH:2 * CH], KSB[:, CH:2 * CH],
                              start=True, stop=True), "t")
            st["t_A"] = cnt["t"]
            yield
            @blk.vector
            def _(v):
                v.wait_ge(s_t, st["t_A"])
                for j in (0, 1):
                    g = pD[:, 256 + j * 128:384 + j * 128]
                    sl1 = slice(j * 128, (j + 1) * 128)
                    v.tensor_scalar(TMPA[:, sl1], g, LR, None, OP.mult)
                tick(v.tensor_copy(RB16[:, 0:1], RB16[:, 0:1]), "v")
                v.wait_ge(s_v, cnt["v"])
                for j in (0, 1):
                    sl1 = slice(j * 128, (j + 1) * 128)
                    sl2 = slice(j * 128, (j + 1) * 128)
                    v.tensor_tensor(TMPB[:, sl1], TMPA[:, sl1], ML[:], OP.mult)
                    v.tensor_tensor(TMPA[:, sl1], TMPA[:, sl1], MU[:], OP.mult)
                tick(v.tensor_copy(RB16[:, 0:1], RB16[:, 0:1]), "v")
                v.wait_ge(s_v, cnt["v"])
                for j in (0, 1):
                    sl1 = slice(j * 128, (j + 1) * 128)
                    v.tensor_tensor(TB[:, sl1], TMPB[:, sl1], I128F[:], OP.add)
                    v.tensor_tensor(MSB[:, sl1], I128F[:], TMPA[:, sl1], OP.subtract)
                    i = v.tensor_tensor(MTSB[:, sl1], I128F[:], TMPB[:, sl1], OP.subtract)
                tick(i, "v")
            prev = ("v", cnt["v"])
            yield
            for it in range(6):
                @blk.tensor
                def _(t):
                    t.wait_ge(SEMS[prev[0]], prev[1])
                    t.matmul(pD[:, 256:384], TB[:, 0:128], MSB[:, 0:128],
                             start=True, stop=True)
                    tick(t.matmul(pD[:, 384:512], TB[:, 128:256], MSB[:, 128:256],
                                  start=True, stop=True), "t")
                st["t_G"] = cnt["t"]
                yield
                @blk.scalar
                def _(s):
                    s.wait_ge(s_t, st["t_G"])
                    s.activation(HB[:, 0:128], pD[:, 256:384], AF.Copy,
                                 bias=0.0, scale=-1.0)
                    tick(s.activation(HB[:, 128:256], pD[:, 384:512], AF.Copy,
                                      bias=0.0, scale=-1.0), "s")
                st["s_Gn"] = cnt["s"]
                @blk.tensor
                def _(t):
                    t.wait_ge(s_s, st["s_Gn"])
                    t.matmul(pD[:, 0:128], TWOI[:], MSB[:, 0:128],
                             start=True, stop=False)
                    t.matmul(pD[:, 0:128], MTSB[:, 0:128], HB[:, 0:128],
                             start=False, stop=True)
                    t.matmul(pD[:, 128:256], TWOI[:], MTSB[:, 0:128],
                             start=True, stop=False)
                    t.matmul(pD[:, 128:256], HB[:, 0:128], MTSB[:, 0:128],
                             start=False, stop=True)
                    t.matmul(pD[:, 256:384], TWOI[:], MSB[:, 128:256],
                             start=True, stop=False)
                    t.matmul(pD[:, 256:384], MTSB[:, 128:256], HB[:, 128:256],
                             start=False, stop=True)
                    t.matmul(pD[:, 384:512], TWOI[:], MTSB[:, 128:256],
                             start=True, stop=False)
                    tick(t.matmul(pD[:, 384:512], HB[:, 128:256], MTSB[:, 128:256],
                                  start=False, stop=True), "t")
                st["t_M"] = cnt["t"]
                yield
                @blk.scalar
                def _(s):
                    s.wait_ge(s_t, st["t_M"])
                    s.activation(MSB[:, 0:128], pD[:, 0:128], AF.Copy, bias=0.0, scale=1.0)
                    s.activation(MTSB[:, 0:128], pD[:, 128:256], AF.Copy, bias=0.0, scale=1.0)
                    s.activation(MSB[:, 128:256], pD[:, 256:384], AF.Copy, bias=0.0, scale=1.0)
                    tick(s.activation(MTSB[:, 128:256], pD[:, 384:512], AF.Copy,
                                      bias=0.0, scale=1.0), "s")
                prev = ("s", cnt["s"])
            yield
            for j in (0, 1):
                par = PX[j]
                cidx = cidxs[j]
                @blk.tensor
                def _(t):
                    t.wait_ge(SEMS[prev[0]], prev[1])
                    if j == 1:
                        t.wait_ge(s_s, span_s[0])
                        t.wait_ge(s_v, st[("v_y", 0)])
                    m = MSB[:, par * 128:(par + 1) * 128]
                    t.matmul(pC[0:CH, 0:512], m, VSB[:, par * D:par * D + 512],
                             start=True, stop=True)
                    t.matmul(pC[0:CH, 512:1024], m, VSB[:, par * D + 512:par * D + 1024],
                             start=True, stop=True)
                    t.matmul(pD[0:CH, 0:DS], m, KTM[:, par * DS:(par + 1) * DS],
                             start=True, stop=True)
                    tick(t.matmul(pD[:, 256 + par * 128:384 + par * 128],
                                  KSB[:, par * CH:(par + 1) * CH],
                                  QSB[:, par * CH:(par + 1) * CH],
                                  start=True, stop=True), "t")
                st[("t_U", j)] = cnt["t"]
                yield
                @blk.scalar
                def _(s):
                    s.wait_ge(s_t, st[("t_U", j)])
                    s.activation(UV[:, par * D:par * D + 512], pC[0:CH, 0:512],
                                 AF.Copy, bias=0.0, scale=LR)
                    s.activation(UV[:, par * D + 512:par * D + 1024], pC[0:CH, 512:1024],
                                 AF.Copy, bias=0.0, scale=LR)
                    tick(s.activation(UK[:, par * DS:(par + 1) * DS],
                                      pD[0:CH, 0:DS], AF.Copy, bias=0.0,
                                      scale=LR), "s")
                st[("s_U", j)] = cnt["s"]
                @blk.vector
                def _(v):
                    v.wait_ge(s_t, st[("t_U", j)])
                    tick(v.tensor_tensor(WTT[:, par * 128:(par + 1) * 128],
                                         pD[:, 256 + par * 128:384 + par * 128],
                                         MU[:], OP.mult), "v")
                st[("v_W", j)] = cnt["v"]
                yield
                for wave in range(2):
                    @blk.tensor
                    def _(t):
                        t.wait_ge(s_s, st[("s_U", j)])
                        t.wait_ge(s_v, st[("v_W", j)] if wave == 0 else st[("v_y", j)])
                        for q in range(4):
                            kd = wave * 4 + q
                            i = t.matmul(pC[:, q * 128:(q + 1) * 128],
                                         UV[:, par * D + kd * 128:par * D + (kd + 1) * 128],
                                         WTT[:, par * 128:(par + 1) * 128],
                                         start=True, stop=True)
                        tick(i, "t")
                    st[("t_y", j)] = cnt["t"]
                    yield
                    @blk.vector
                    def _(v):
                        v.wait_ge(s_t, st[("t_y", j)])
                        v.wait_ge(s_sp, rp_prev[0])
                        for q in range(4):
                            kd = wave * 4 + q
                            i = v.tensor_tensor(RP[:, csls[j][kd]], RP[:, csls[j][kd]],
                                                pC[:, q * 128:(q + 1) * 128], OP.add)
                        tick(i, "v")
                    st[("v_y", j)] = cnt["v"]
                yield
                @blk.tensor
                def _(t):
                    t.wait_ge(s_v, st[("v_y", j)])
                    if span_s[0]:
                        t.wait_ge(s_s, span_s[0])
                    t.matmul(pD[0:DS, 0:128], UK[:, par * DS:(par + 1) * DS],
                             WTT[:, par * 128:(par + 1) * 128], start=True, stop=True)
                    t.matmul(pD[0:DS, 128:128 + DS], UK[:, par * DS:(par + 1) * DS],
                             KTM[:, par * DS:(par + 1) * DS], start=True, stop=True)
                    t.matmul(pC[0:DS, 0:512], KTM[:, par * DS:(par + 1) * DS],
                             UV[:, par * D:par * D + 512], start=True, stop=True)
                    tick(t.matmul(pC[0:DS, 512:1024], KTM[:, par * DS:(par + 1) * DS],
                                  UV[:, par * D + 512:par * D + 1024],
                                  start=True, stop=True), "t")
                st[("t_QP", j)] = cnt["t"]
                yield
                @blk.vector
                def _(v):
                    v.wait_ge(s_t, st[("t_QP", j)])
                    v.wait_ge(s_sp, qhct_prev[0])
                    v.scalar_tensor_tensor(QH[:], pD[0:DS, 0:128], -1.0,
                                           QSB[:, par * CH:(par + 1) * CH],
                                           OP.mult, OP.add)
                    tick(v.scalar_tensor_tensor(PC[:, cidx * DS:(cidx + 1) * DS],
                                                pD[0:DS, 128:128 + DS], -1.0, I64F[:],
                                                OP.mult, OP.add), "v")
                st[("v_QP", j)] = cnt["v"]
                @blk.scalar
                def _(s):
                    s.wait_ge(s_t, st[("t_QP", j)])
                    s.wait_ge(s_sp, qhct_prev[0])
                    s.activation(CTC[:, 0:512], pC[0:DS, 0:512], AF.Copy,
                                 bias=0.0, scale=1.0)
                    tick(s.activation(CTC[:, 512:1024], pC[0:DS, 512:1024], AF.Copy,
                                      bias=0.0, scale=1.0), "s")
                st[("s_ctc", j)] = cnt["s"]
                yield
                @blk.gpsimd
                def _(g):
                    g.wait_ge(s_v, st[("v_QP", j)])
                    g.wait_ge(s_s, st[("s_ctc", j)])
                    g.wait_ge(s_sp, cnt["sp"])
                    tick(g.dma_start(out=spqh_d[cidx], in_=QH[:]), "sp", 16)
                    tick(g.dma_start(out=spct_d[cidx], in_=CTC[:]), "sp", 16)
                qhct_prev[0] = cnt["sp"]
                @blk.tensor
                def _(t):
                    t.wait_ge(s_v, st[("v_QP", j)])
                    t.wait_ge(s_s, st[("s_ctc", j)])
                    pcs = PC[:, cidx * DS:(cidx + 1) * DS]
                    t.matmul(pD[0:DS, 128:128 + DS], SPTB[:], pcs, start=True, stop=True)
                    t.matmul(pD[0:DS, 192:192 + DS], pcs, SPTB[:], start=True, stop=True)
                    t.matmul(pC[0:DS, 0:512], pcs, SCTB[:, 0:512], start=True, stop=False)
                    t.matmul(pC[0:DS, 0:512], I64H[:], CTC[:, 0:512], start=False, stop=True)
                    t.matmul(pC[0:DS, 512:1024], pcs, SCTB[:, 512:1024], start=True, stop=False)
                    tick(t.matmul(pC[0:DS, 512:1024], I64H[:], CTC[:, 512:1024],
                                  start=False, stop=True), "t")
                st[("t_span", j)] = cnt["t"]
                yield
                @blk.scalar
                def _(s):
                    s.wait_ge(s_t, st[("t_span", j)])
                    s.activation(SPB[:], pD[0:DS, 128:128 + DS], AF.Copy, bias=0.0, scale=1.0)
                    s.activation(SPTB[:], pD[0:DS, 192:192 + DS], AF.Copy, bias=0.0, scale=1.0)
                    s.activation(SCTB[:, 0:512], pC[0:DS, 0:512], AF.Copy, bias=0.0, scale=1.0)
                    tick(s.activation(SCTB[:, 512:1024], pC[0:DS, 512:1024], AF.Copy,
                                      bias=0.0, scale=1.0), "s")
                span_s[0] = cnt["s"]
                if j == 0:
                    yield
            if pidx == NCH // 2 - 1:
                yield
                @blk.gpsimd
                def _(g):
                    g.wait_ge(s_v, st[("v_y", 1)])
                    g.wait_ge(s_sp, cnt["sp"])
                    for kd in range(KD):
                        tick(g.dma_start(out=sprp_d[kd, :, tt * TT:(tt + 1) * TT],
                                         in_=RP[:, kd * TT:(kd + 1) * TT]), "sp", 16)
                rp_tile_spill[tt] = cnt["sp"]
                rp_prev[0] = cnt["sp"]

        pending = []
        def pump():
            if pending:
                try:
                    next(pending[0])
                except StopIteration:
                    pending.pop(0)
        def pump_all():
            while pending:
                try:
                    next(pending[0])
                except StopIteration:
                    pending.pop(0)

        ln2_prev = [init_v]
        for tt in range(NTILE):
            @blk.sync
            def _(sy):
                if tt == 0:
                    sy.wait_ge(s_w, w_done)
                    sy.wait_ge(s_wg, wg_done)
                    sy.wait_ge(s_v, init_v)
                else:
                    sy.wait_ge(s_v, ln2_prev[0])
                    sy.wait_ge(s_x, cnt["x"])
                for kd in range(KD):
                    tick(sy.dma_start(out=XF[:, kd * TT:(kd + 1) * TT],
                                      in_=x_d[kd, :, tt * TT:(tt + 1) * TT]), "x", 16)
            x_loaded = cnt["x"]

            st1 = emit_ln_stats([("x", x_loaded)], XF, pump=pump)
            a0 = emit_ln_tail("q", 0, st1)
            a1 = emit_ln_tail("q", 1, st1)
            ffn1_v = emit_ffn(tt, a1, XF, True, pump=pump)
            pump_all()
            st2 = emit_ln_stats([("v", ffn1_v)], XF)
            b0 = emit_ln_tail("ln2", 0, st2, apply_waits=[("sp", rp_prev[0])])
            pending.append(pair_stages_p1(tt, 0, b0))
            b1 = emit_ln_tail("ln2", 1, st2, apply_waits=[("sp", rp_prev[0])])
            pump(); pump()
            pending.append(pair_stages_p1(tt, 1, b1))
            ln2_prev[0] = b1
            if tt == NTILE - 1:
                pump_all()

        @blk.gpsimd
        def _(g):
            g.wait_ge(s_s, span_s[0])
            g.wait_ge(s_sp, cnt["sp"])
            tick(g.dma_start(out=ccin_d[:, 0:DS], in_=SPB[:]), "sp", 16)
            tick(g.dma_start(out=ccin_d[:, DS:DS + D], in_=SCTB[:]), "sp", 16)
        cc_staged = cnt["sp"]

        @blk.gpsimd
        def _(g):
            g.wait_ge(s_sp, cc_staged)
            if sim1:
                for p in range(W):
                    g.dma_start(out=ccout_d[p * DS:(p + 1) * DS, :],
                                in_=ccin_d[:]).then_inc(s_cc, 16)
            else:
                g.collective_compute(
                    "AllGather", OP.bypass,
                    replica_groups=[list(range(W))],
                    ins=[ccin_d.ap().opt()],
                    outs=[ccout_d.ap().opt()],
                ).then_inc(s_cc, 1)
            g.wait_ge(s_cc, 16 * W if sim1 else 1)
            tick(g.dma_start(out=GB[:], in_=ccout_d[0:DS, :]), "g", 16)
        g_gathered = cnt["g"]

        @blk.vector
        def _(v):
            v.wait_ge(s_g, g_gathered)
            tick(v.memset(ZSB[:], 0.0), "v")
        z_prev = ("v", cnt["v"])
        for p in range(W):
            @blk.vector
            def _(v):
                v.wait_ge(s_g, g_gathered + p * 16)
                v.wait_ge(SEMS[z_prev[0]], z_prev[1])
                v.scalar_tensor_tensor(PBL[:], GB[:, 0:DS],
                                       LAM[:, p:p + 1], LAMI[:, p * DS:(p + 1) * DS],
                                       OP.mult, OP.add)
                tick(v.tensor_scalar(CBL[:], GB[:, DS:DS + D],
                                     LAM[:, p:p + 1], None, OP.mult), "v")
            v_bl = cnt["v"]
            if p + 1 < W:
                @blk.gpsimd
                def _(g):
                    g.wait_ge(s_v, v_bl)
                    tick(g.dma_start(out=GB[:],
                                     in_=ccout_d[(p + 1) * DS:(p + 2) * DS, :]), "g", 16)
            @blk.tensor
            def _(t):
                t.wait_ge(s_v, v_bl)
                t.matmul(pC[0:DS, 0:512], PBL[:], ZSB[:, 0:512], start=True, stop=False)
                t.matmul(pC[0:DS, 0:512], I64H[:], CBL[:, 0:512], start=False, stop=True)
                t.matmul(pC[0:DS, 512:1024], PBL[:], ZSB[:, 512:1024], start=True, stop=False)
                tick(t.matmul(pC[0:DS, 512:1024], I64H[:], CBL[:, 512:1024],
                              start=False, stop=True), "t")
            t_z = cnt["t"]
            @blk.scalar
            def _(s):
                s.wait_ge(s_t, t_z)
                s.activation(ZSB[:, 0:512], pC[0:DS, 0:512], AF.Copy, bias=0.0, scale=1.0)
                tick(s.activation(ZSB[:, 512:1024], pC[0:DS, 512:1024], AF.Copy,
                                  bias=0.0, scale=1.0), "s")
            z_prev = ("s", cnt["s"])

        def zslot(tt, ch):
            if ch == 0:
                return ZSB if tt % 2 == 0 else ZC1
            return ZH[:, (ch - 1) * D: ch * D]

        ct_consumed = [0, 0]
        zstate = {"prev": z_prev}

        def chain_stages_p2(tt, ch):
            cidx = tt * NCH + ch
            slot = (cidx % 2) * D
            rckey = "rc0" if cidx % 2 == 0 else "rc1"
            src = zslot(tt, ch)
            dst = zslot(tt, ch + 1) if ch + 1 < NCH else zslot(tt + 1, 0)
            st = {}
            zp = zstate["prev"]
            @blk.sync
            def _(sy):
                sy.wait_ge(s_t, ct_consumed[cidx % 2])
                if tt == 0 and ch == 0:
                    sy.wait_ge(SEMS[zp[0]], zp[1])
                tick(sy.dma_start(out=CTL[:, slot:slot + D], in_=spct_d[cidx]), rckey, 16)
            r_ct = (rckey, cnt[rckey])
            yield
            @blk.tensor
            def _(t):
                t.wait_ge(SEMS[zp[0]], zp[1])
                t.wait_ge(SEMS[r_ct[0]], r_ct[1])
                pcs = PC[:, cidx * DS:(cidx + 1) * DS]
                t.matmul(pC[0:DS, 0:512], pcs, src[:, 0:512], start=True, stop=False)
                t.matmul(pC[0:DS, 0:512], I64H[:], CTL[:, slot:slot + 512],
                         start=False, stop=True)
                t.matmul(pC[0:DS, 512:1024], pcs, src[:, 512:1024], start=True, stop=False)
                tick(t.matmul(pC[0:DS, 512:1024], I64H[:], CTL[:, slot + 512:slot + D],
                              start=False, stop=True), "t")
            ct_consumed[cidx % 2] = cnt["t"]
            st["t_zc"] = cnt["t"]
            yield
            @blk.scalar
            def _(s):
                s.wait_ge(s_t, st["t_zc"])
                s.activation(dst[:, 0:512], pC[0:DS, 0:512], AF.Copy, bias=0.0, scale=1.0)
                tick(s.activation(dst[:, 512:1024], pC[0:DS, 512:1024], AF.Copy,
                                  bias=0.0, scale=1.0), "s")
            zstate["prev"] = ("s", cnt["s"])


        def emit_ycorr(tt, r_loaded, prev_ffn_v):
            marks = {}
            t_pd_free = cnt["t"]
            for ch in range(NCH):
                csl = [slice(kd * TT + ch * CH, kd * TT + (ch + 1) * CH)
                       for kd in range(KD)]
                src = zslot(tt, ch)
                for wave in range(2):
                    @blk.tensor
                    def _(t):
                        if ch == 0 and wave == 0:
                            t.wait_ge(s_r, r_loaded)
                            t.wait_ge(SEMS[zstate["prev"][0]], zstate["prev"][1])
                            t.wait_ge(s_v, cnt["v"])
                        else:
                            t.wait_ge(s_v, marks["x2"])
                        for j in range(4):
                            kd = wave * 4 + j
                            i = t.matmul(pD[:, j * 128:(j + 1) * 128],
                                         src[:, kd * 128:(kd + 1) * 128],
                                         QHL[:, ch * CH:(ch + 1) * CH],
                                         start=True, stop=True)
                        tick(i, "t")
                    t_y = cnt["t"]
                    @blk.vector
                    def _(v):
                        v.wait_ge(s_t, t_y)
                        if ch == 0 and wave == 0:
                            v.wait_ge(s_v, prev_ffn_v)
                            v.wait_ge(s_sp, rp_tile_spill[NTILE - 1])
                        for j in range(4):
                            kd = wave * 4 + j
                            i = v.tensor_tensor(RP[:, csl[kd]], HA[:, csl[kd]],
                                                pD[:, j * 128:(j + 1) * 128], OP.add)
                        tick(i, "v")
                    marks["x2"] = cnt["v"]
            return marks["x2"]

        prev_ffn2_v = [0]
        prev_out = [0]
        p2_reload = [0] * NTILE

        def emit_reload(tt):
            @blk.sync
            def _(sy):
                sy.wait_ge(s_sp, rp_tile_spill[tt])
                sy.wait_ge(s_r, cnt["r"])
                if tt > 0:
                    sy.wait_ge(s_v, p2_reload_guard[0])
                else:
                    sy.wait_ge(s_t, cnt["t"])
                    sy.wait_ge(s_v, cnt["v"])
                for kd in range(KD):
                    tick(sy.dma_start(out=HA[:, kd * TT:(kd + 1) * TT],
                                      in_=sprp_d[kd, :, tt * TT:(tt + 1) * TT]), "r", 16)
                for ch in range(NCH):
                    tick(sy.dma_start(out=QHL[:, ch * CH:(ch + 1) * CH],
                                      in_=spqh_d[tt * NCH + ch]), "r", 16)
            p2_reload[tt] = cnt["r"]

        p2_reload_guard = [0]
        emit_reload(0)
        for ch in range(NCH):
            pending.append(chain_stages_p2(0, ch))
        pump_all()

        for tt in range(NTILE):
            x2_v = emit_ycorr(tt, p2_reload[tt], prev_ffn2_v[0])
            p2_reload_guard[0] = x2_v
            st3 = emit_ln_stats([("v", x2_v)], RP, pump=pump)
            c0 = emit_ln_tail("q", 0, st3)
            if tt + 1 < NTILE:
                emit_reload(tt + 1)
                for ch in range(NCH):
                    pending.append(chain_stages_p2(tt + 1, ch))
            c1 = emit_ln_tail("q", 1, st3)
            ffn2_v = emit_ffn(tt, c1, RP, False,
                              evac_waits=[("o", prev_out[0])], pump=pump)
            pump_all()
            prev_ffn2_v[0] = ffn2_v


            @blk.sync
            def _(sy):
                sy.wait_ge(s_v, ffn2_v)
                for kd in range(KD):
                    tick(sy.dma_start(out=out_d[kd, :, tt * TT:(tt + 1) * TT],
                                      in_=XF[:, kd * TT:(kd + 1) * TT]), "o", 16)
            prev_out[0] = cnt["o"]

        @blk.sync
        def _(sy):
            sy.wait_ge(s_o, cnt["o"])
    return nc


def threefry2x32_np(k0, k1, c0, c1):
    ks0, ks1 = np.uint32(k0), np.uint32(k1)
    ks2 = np.uint32(ks0 ^ ks1 ^ np.uint32(0x1BD11BDA))
    rot = [[13, 15, 26, 6], [17, 29, 16, 24]]
    def rotl(x, d):
        return ((x << np.uint32(d)) | (x >> np.uint32(32 - d))).astype(np.uint32)
    x0 = (c0 + ks0).astype(np.uint32); x1 = (c1 + ks1).astype(np.uint32)
    ks = [ks0, ks1, ks2]
    for i in range(5):
        r = rot[0] if i % 2 == 0 else rot[1]
        for j in range(4):
            x0 = (x0 + x1).astype(np.uint32)
            x1 = rotl(x1, r[j])
            x1 = (x1 ^ x0).astype(np.uint32)
        x0 = (x0 + ks[(i + 1) % 3]).astype(np.uint32)
        x1 = (x1 + ks[(i + 2) % 3] + np.uint32(i + 1)).astype(np.uint32)
    return x0, x1


def host_prep(inp):
    x = np.ascontiguousarray(np.asarray(inp["L0_attn_x"], np.float32))
    B, S, Dm = x.shape
    xf = x.reshape(B * S, Dm)
    n = B * S * Dm
    c2 = np.arange(n, dtype=np.uint32)
    c1 = np.zeros(n, np.uint32)
    a0, a1 = threefry2x32_np(0, 42, c1, c2)
    bits = (a0 ^ a1).astype(np.uint32)
    f = ((bits >> np.uint32(9)) | np.uint32(0x3F800000)).view(np.float32) - np.float32(1.0)
    keep = np.maximum(np.float32(0.0), f) < np.float32(0.8)
    mask = np.where(keep.reshape(B * S, Dm), np.float16(0.625), np.float16(0.0))

    w1 = np.asarray(inp["L0_ffn_w1"], np.float32)
    w2 = np.asarray(inp["L0_ffn_w2"], np.float32)
    wv = np.asarray(inp["L0_attn_Wv"], np.float32)
    wk = np.asarray(inp["L0_attn_Wk"], np.float32)
    wq = np.asarray(inp["L0_attn_Wq"], np.float32)
    w1t = np.ascontiguousarray(w1.T.reshape(KD, 128, Hh))
    w2t = np.ascontiguousarray(w2.T.reshape(KH, 128, D))
    wvt = np.ascontiguousarray(wv.T.reshape(KD, 128, D))
    wkt = np.ascontiguousarray(wk.T.reshape(KD, 128, DS))
    wqt = np.ascontiguousarray(wq.T.reshape(KD, 128, DS))
    bias_h = np.ascontiguousarray(np.asarray(inp["L0_ffn_b1"], np.float32)
                                  .reshape(KH, 128).T)
    vecd = np.zeros((128, 24), np.float32)
    vecd[:, 0:8] = np.asarray(inp["L0_ffn_b2"], np.float32).reshape(KD, 128).T
    vecd[:, 8:16] = np.asarray(inp["L0_attn_gamma"], np.float32).reshape(KD, 128).T
    vecd[:, 16:24] = np.asarray(inp["L0_attn_beta"], np.float32).reshape(KD, 128).T
    bias_e = np.stack([np.asarray(inp["L0_attn_bk"], np.float32),
                       np.asarray(inp["L0_attn_bq"], np.float32)], axis=1)
    bvrow = np.asarray(inp["L0_attn_bv"], np.float32).reshape(1, D)
    bkrow = np.asarray(inp["L0_attn_bk"], np.float32).reshape(1, DS)
    consts = np.zeros((128, 576), np.float32)
    consts[:, 0:128] = np.tril(np.ones((128, 128), np.float32), -1)
    consts[:, 128:256] = np.triu(np.ones((128, 128), np.float32), 1)
    consts[:, 256:384] = 2.0 * np.eye(128, dtype=np.float32)
    consts[:, 384:512] = np.eye(128, dtype=np.float32)
    consts[0:DS, 512:576] = np.eye(DS, dtype=np.float32)
    i64h = np.eye(DS, dtype=np.float16)

    shared = dict(w1t=w1t, w2t=w2t, wvt=wvt, wkt=wkt, wqt=wqt, bias_h=bias_h,
                  vecd=vecd, bias_e=bias_e, bvrow=bvrow, bkrow=bkrow,
                  consts=consts, i64h=i64h)
    in_maps = []
    spanlen = B * S // W
    for c in range(W):
        sl = slice(c * spanlen, (c + 1) * spanlen)
        xs = np.ascontiguousarray(xf[sl].T.reshape(KD, 128, SPAN))
        ms = np.ascontiguousarray(mask[sl].T.reshape(KD, 128, SPAN))
        lam = np.zeros((DS, W), np.float32)
        g0 = 4 * (c // 4)
        for p in range(g0, c):
            lam[:, p] = 1.0
        lamI = np.zeros((W, DS, DS), np.float16)
        for p in range(W):
            lamI[p] = np.eye(DS, dtype=np.float16) * (0.0 if g0 <= p < c else 1.0)
        lamI = np.ascontiguousarray(lamI.transpose(1, 0, 2).reshape(DS, W * DS))
        m = dict(shared)
        m["x"] = xs
        m["mask"] = ms
        m["lam"] = lam
        m["lamI"] = lamI
        in_maps.append(m)
    return in_maps


def gather_out(results):
    outs = []
    for c in range(W):
        o = results[c]["out"]
        outs.append(np.ascontiguousarray(o.reshape(D, SPAN).T))
    return np.concatenate(outs, axis=0)


def kernel(**inputs):
    import concourse.bass_utils as bass_utils
    nc = build_program("full")
    in_maps = host_prep(inputs)
    res = bass_utils.run_bass_kernel_spmd(nc, in_maps, core_ids=list(range(W)))
    out = gather_out(res.results)
    return out.reshape(2, 16384, 1024)
